# version 07
# baseline (speedup 1.0000x reference)
import os
os.environ.setdefault("TRN_ALLOW_ICI_ALL_TO_ALL", "1")
import math
import numpy as np
import ml_dtypes
from contextlib import ExitStack
import concourse.bass as bass
import concourse.mybir as mybir
from concourse.bass_utils import run_bass_kernel_spmd

F32 = mybir.dt.float32
BF16 = mybir.dt.bfloat16
AF = mybir.ActivationFunctionType
ALU = mybir.AluOpType
bfdt = ml_dtypes.bfloat16
AX = mybir.AxisListType.X

NCORE = 8
B, L, DM = 4, 4096, 1024
PL = L // NCORE
T = B * PL
NT = T // 128
ENGS = ("sync", "vector", "tensor", "scalar", "gpsimd")


class KB:

    def __init__(self, name="k"):
        self.nc = bass.Bass("TRN2", target_bir_lowering=False)
        self.st = ExitStack()
        self.stk = [self.st]
        self.q = {e: [] for e in ENGS}
        self.cnt = {}
        self._sem = {}
        self.name = name

    def scope(self):
        from contextlib import contextmanager
        @contextmanager
        def _s():
            st = ExitStack()
            self.stk.append(st)
            try:
                yield
            finally:
                self.stk.pop()
                st.close()
        return _s()

    def barrier(self, extra=()):
        snap = tuple((s, self.cnt[s]) for s in ("st", "sv", "ss", "sg", "sd", "sp")
                     if self.cnt.get(s, 0) > 0) + tuple(extra)
        for eng in ENGS:
            self.on(eng, lambda e: None, wait=snap)
        return snap

    def sem(self, name):
        if name not in self._sem:
            self._sem[name] = self.st.enter_context(self.nc.semaphore(name))
            self.cnt[name] = 0
        return self._sem[name]

    _uid = 0

    def sbuf(self, name, shape, dt=F32):
        KB._uid += 1
        return self.stk[-1].enter_context(self.nc.sbuf_tensor(f"{name}_{KB._uid}", shape, dt))

    def psum(self, name, shape, dt=F32):
        return self.st.enter_context(self.nc.psum_tensor(name, shape, dt))

    def din(self, name, shape, dt=F32):
        return self.nc.dram_tensor(name, shape, dt, kind="ExternalInput")

    def dout(self, name, shape, dt=F32):
        return self.nc.dram_tensor(name, shape, dt, kind="ExternalOutput")

    def dtmp(self, name, shape, dt=F32):
        return self.nc.dram_tensor(name, shape, dt)

    def on(self, eng, fn, wait=(), inc=()):
        for s, n in inc:
            self.sem(s)
            self.cnt[s] += n
        for s, _ in wait:
            self.sem(s)
        self.q[eng].append((tuple(wait), fn, tuple(inc)))

    def v(self, fn, wait=()):
        self.on("vector", fn, wait, (("sv", 1),)); return self.cnt["sv"]

    def s(self, fn, wait=()):
        self.on("scalar", fn, wait, (("ss", 1),)); return self.cnt["ss"]

    def t(self, fn, wait=()):
        self.on("tensor", fn, wait, (("st", 1),)); return self.cnt["st"]

    def g(self, fn, wait=()):
        self.on("gpsimd", fn, wait, (("sp", 1),)); return self.cnt["sp"]

    def dma(self, eng, out, in_, wait=(), sem=None, **kw):
        sem = sem or ("sd" if eng in ("sync", "scalar") else "sg")
        self.on(eng, (lambda e, o=out, i=in_, kw=kw: e.dma_start(out=o, in_=i, **kw)),
                wait=wait, inc=((sem, 16),))
        return self.cnt[sem]

    def c(self, s):
        return self.cnt.get(s, 0)

    def build(self):
        blk = self.st.enter_context(self.nc.Block())
        for eng in ENGS:
            ops = self.q[eng]
            if not ops:
                continue
            drained = eng in ("vector", "scalar", "gpsimd")
            def runner(e, ops=ops, drained=drained):
                ep = DrainProxy(e) if drained else e
                for wait, fn, inc in ops:
                    for s, vv in wait:
                        e.wait_ge(self._sem[s], vv)
                    instr = fn(ep)
                    if inc:
                        assert instr is not None
                        if drained and ep.n_compute > 0:
                            instr = e.drain()
                            ep.n_compute = 0
                        for s, n in inc:
                            instr.then_inc(self._sem[s], n)
            getattr(blk, eng)(runner)
        self.st.close()
        return self.nc


_DRAIN_AFTER = {
    "tensor_tensor", "tensor_scalar", "tensor_copy", "tensor_reduce",
    "tensor_add", "tensor_sub", "tensor_mul", "tensor_max", "tensor_relu",
    "tensor_scalar_add", "tensor_scalar_mul", "tensor_scalar_max",
    "tensor_scalar_min", "tensor_scalar_sub", "scalar_tensor_tensor",
    "reduce_max", "reduce_sum", "reciprocal", "memset", "select",
    "activation", "copy", "add", "mul", "sign", "sqrt", "square", "memzero",
    "iota", "tensor_single_scalar", "max", "max_index", "max_with_indices",
}


class DrainProxy:

    def __init__(self, e):
        self._e = e
        self.n_compute = 0

    def __getattr__(self, name):
        attr = getattr(self._e, name)
        if name in _DRAIN_AFTER:
            def wrapped(*a, **kw):
                r = attr(*a, **kw)
                self._e.drain()
                self.n_compute += 1
                return r
            return wrapped
        return attr



class Common:

    def __init__(self, k: KB):
        self.k = k
        self.ident = k.sbuf("c_ident", [128, 128])
        self.identb = k.sbuf("c_identb", [128, 128], BF16)
        self.ones_col = k.sbuf("c_ones_col", [128, 1])
        self.ones_row = k.sbuf("c_ones_row", [1, 128])
        self.ones_rowb = k.sbuf("c_ones_rowb", [1, 128], BF16)
        self.pp = [k.psum(f"c_pp{i}", [128, 512]) for i in range(6)]
        self.prow = k.psum("c_prow", [1, 512])
        self.pbf = k.psum("c_pbf", [128, 1024], BF16)
        identI = k.din("identI", [128, 128], F32)
        k.dma("sync", self.ident[:, :], identI[:, :])
        self.ld = k.c("sd")
        k.v(lambda e: e.tensor_copy(self.identb[:, :], self.ident[:, :]),
            wait=(("sd", self.ld),))
        def gm(e):
            e.memset(self.ones_col[:, :], 1.0)
            e.memset(self.ones_row[:, :], 1.0)
            return e.memset(self.ones_rowb[:, :], 1.0)
        k.g(gm)
        self.ready = {"sv": k.c("sv"), "sp": k.c("sp")}

    def wait_ready(self):
        return (("sv", self.ready["sv"]), ("sp", self.ready["sp"]))


def emit_ln_fm(k, cm, src, nch, dst=None, dstb=None, wrow=None, brow=None,
               wait=(), eps=1e-5, sq_buf=None, rows=None, mb=None, rb=None,
               apply_wait=()):
    NB = T // 512
    inv = 1.0 / (nch * 128)
    first = True
    for cb in range(NB):
        def t_sum(e, cb=cb):
            r = None
            for ch in range(nch):
                r = e.matmul(cm.prow[0:1, :], cm.ones_col[:, 0:1],
                             src[:, ch * T + cb * 512: ch * T + cb * 512 + 512],
                             start=(ch == 0), stop=(ch == nch - 1))
            return r
        w = (wait + cm.wait_ready()) if first else (("sv", k.c("sv")),)
        first = False
        k.t(t_sum, wait=w)
        k.v(lambda e, cb=cb: e.tensor_copy(rows[0:1, cb * 512:(cb + 1) * 512], cm.prow[0:1, :]),
            wait=(("st", k.c("st")),))
        for ch in range(nch):
            k.s(lambda e, ch=ch, cb=cb: e.activation(
                sq_buf[:, :], src[:, ch * T + cb * 512: ch * T + cb * 512 + 512], AF.Square),
                wait=(("st", k.c("st")), ("sv", k.c("sv"))))
            k.t(lambda e, ch=ch: e.matmul(cm.prow[0:1, :], cm.ones_col[:, 0:1],
                                          sq_buf[:, :], start=(ch == 0), stop=(ch == nch - 1)),
                wait=(("ss", k.c("ss")),))
        k.v(lambda e, cb=cb: e.tensor_copy(rows[0:1, T + cb * 512: T + cb * 512 + 512], cm.prow[0:1, :]),
            wait=(("st", k.c("st")),))
    def v_rows(e):
        e.tensor_scalar(rows[0:1, 0:T], rows[0:1, 0:T], inv, None, ALU.mult)
        e.tensor_scalar(rows[0:1, T:2 * T], rows[0:1, T:2 * T], inv, None, ALU.mult)
        e.tensor_tensor(rows[0:1, 2 * T:3 * T], rows[0:1, 0:T], rows[0:1, 0:T], ALU.mult)
        e.tensor_tensor(rows[0:1, T:2 * T], rows[0:1, T:2 * T], rows[0:1, 2 * T:3 * T], ALU.subtract)
        return e.tensor_scalar(rows[0:1, T:2 * T], rows[0:1, T:2 * T], eps, None, ALU.add)
    k.v(v_rows)
    k.s(lambda e: e.activation(rows[0:1, T:2 * T], rows[0:1, T:2 * T], AF.Sqrt),
        wait=(("sv", k.c("sv")),))
    k.v(lambda e: e.reciprocal(rows[0:1, T:2 * T], rows[0:1, T:2 * T]),
        wait=(("ss", k.c("ss")),))
    vrow = k.c("sv")
    for cb in range(NB):
        def t_bc(e, cb=cb):
            e.matmul(cm.pp[4][:, :], cm.ones_row[0:1, :], rows[0:1, cb * 512:(cb + 1) * 512],
                     start=True, stop=True)
            return e.matmul(cm.pp[5][:, :], cm.ones_row[0:1, :], rows[0:1, T + cb * 512:T + cb * 512 + 512],
                            start=True, stop=True)
        k.t(t_bc, wait=(("sv", k.c("sv")),))
        def v_bc(e, cb=cb):
            e.tensor_copy(mb[:, cb * 512:(cb + 1) * 512], cm.pp[4][:, :])
            return e.tensor_copy(rb[:, cb * 512:(cb + 1) * 512], cm.pp[5][:, :])
        k.v(v_bc, wait=(("st", k.c("st")),))
    vbc = k.c("sv")
    out = dst if dst is not None else src
    for ch in range(nch):
        def v_ap(e, ch=ch):
            e.tensor_tensor(out[:, ch * T:(ch + 1) * T], src[:, ch * T:(ch + 1) * T], mb[:, 0:T], ALU.subtract)
            r = e.tensor_tensor(out[:, ch * T:(ch + 1) * T], out[:, ch * T:(ch + 1) * T], rb[:, 0:T], ALU.mult)
            if wrow is not None:
                r = e.tensor_scalar(out[:, ch * T:(ch + 1) * T], out[:, ch * T:(ch + 1) * T],
                                    wrow[:, ch:ch + 1], brow[:, ch:ch + 1], ALU.mult, ALU.add)
            if dstb is not None:
                r = e.tensor_copy(dstb[:, ch * T:(ch + 1) * T], out[:, ch * T:(ch + 1) * T])
            return r
        k.v(v_ap, wait=((("sv", vbc),) + tuple(apply_wait)) if ch == 0 else ())
    return ("sv", k.c("sv"))


def emit_ln_stream(k, cm, src_dram, nch, xwin, dstb, xn_dram=None,
                   wait=(), eps=1e-5, sq_buf=None, rows=None, mb=None, rb=None,
                   dma_eng="sync"):
    inv = 1.0 / (nch * 128)
    sem = "sd" if dma_eng in ("sync", "scalar") else "sg"
    for cb in range(4):
        ld = k.dma(dma_eng, xwin[:, :],
                   bass.AP(src_dram, cb * 512, [[T, 128], [128 * T, nch], [1, 512]]),
                   wait=wait + (("st", k.c("st")),))
        def t_sum(e, cb=cb, ld=ld):
            e.wait_ge(k._sem[sem], ld)
            r = None
            for ch in range(nch):
                r = e.matmul(cm.prow[0:1, :], cm.ones_col[:, 0:1],
                             xwin[:, ch * 512:(ch + 1) * 512],
                             start=(ch == 0), stop=(ch == nch - 1))
            return r
        k.t(t_sum, wait=cm.wait_ready() + (("sv", k.c("sv")),))
        k.v(lambda e, cb=cb: e.tensor_copy(rows[0:1, cb * 512:(cb + 1) * 512], cm.prow[0:1, :]),
            wait=(("st", k.c("st")),))
        for ch in range(nch):
            k.s(lambda e, ch=ch: e.activation(sq_buf[:, :], xwin[:, ch * 512:(ch + 1) * 512], AF.Square),
                wait=(("st", k.c("st")), ("sv", k.c("sv"))))
            k.t(lambda e, ch=ch: e.matmul(cm.prow[0:1, :], cm.ones_col[:, 0:1],
                                          sq_buf[:, :], start=(ch == 0), stop=(ch == nch - 1)),
                wait=(("ss", k.c("ss")),))
        k.v(lambda e, cb=cb: e.tensor_copy(rows[0:1, T + cb * 512: T + cb * 512 + 512], cm.prow[0:1, :]),
            wait=(("st", k.c("st")),))
    def v_rows(e):
        e.tensor_scalar(rows[0:1, 0:T], rows[0:1, 0:T], inv, None, ALU.mult)
        e.tensor_scalar(rows[0:1, T:2 * T], rows[0:1, T:2 * T], inv, None, ALU.mult)
        e.tensor_tensor(mb[0:1, 0:T], rows[0:1, 0:T], rows[0:1, 0:T], ALU.mult)
        e.tensor_tensor(rows[0:1, T:2 * T], rows[0:1, T:2 * T], mb[0:1, 0:T], ALU.subtract)
        return e.tensor_scalar(rows[0:1, T:2 * T], rows[0:1, T:2 * T], eps, None, ALU.add)
    k.v(v_rows)
    k.s(lambda e: e.activation(rows[0:1, T:2 * T], rows[0:1, T:2 * T], AF.Sqrt),
        wait=(("sv", k.c("sv")),))
    k.v(lambda e: e.reciprocal(rows[0:1, T:2 * T], rows[0:1, T:2 * T]),
        wait=(("ss", k.c("ss")),))
    for cb in range(4):
        def t_bc(e, cb=cb):
            e.matmul(cm.pp[4][:, :], cm.ones_row[0:1, :], rows[0:1, cb * 512:(cb + 1) * 512],
                     start=True, stop=True)
            return e.matmul(cm.pp[5][:, :], cm.ones_row[0:1, :],
                            rows[0:1, T + cb * 512:T + cb * 512 + 512], start=True, stop=True)
        k.t(t_bc, wait=(("sv", k.c("sv")),))
        def v_bc(e, cb=cb):
            e.tensor_copy(mb[:, cb * 512:(cb + 1) * 512], cm.pp[4][:, :])
            return e.tensor_copy(rb[:, cb * 512:(cb + 1) * 512], cm.pp[5][:, :])
        k.v(v_bc, wait=(("st", k.c("st")),))
    for cb in range(4):
        ld = k.dma(dma_eng, xwin[:, :],
                   bass.AP(src_dram, cb * 512, [[T, 128], [128 * T, nch], [1, 512]]),
                   wait=(("st", k.c("st")), ("sv", k.c("sv"))) + ((("sg", k.c("sg")),) if xn_dram is not None else ()))
        def v_ap(e, cb=cb, ld=ld):
            e.wait_ge(k._sem[sem], ld)
            r = None
            for ch in range(nch):
                e.tensor_tensor(xwin[:, ch * 512:(ch + 1) * 512], xwin[:, ch * 512:(ch + 1) * 512],
                                mb[:, cb * 512:(cb + 1) * 512], ALU.subtract)
                e.tensor_tensor(xwin[:, ch * 512:(ch + 1) * 512], xwin[:, ch * 512:(ch + 1) * 512],
                                rb[:, cb * 512:(cb + 1) * 512], ALU.mult)
                r = e.tensor_copy(dstb[:, ch * T + cb * 512: ch * T + cb * 512 + 512],
                                  xwin[:, ch * 512:(ch + 1) * 512])
            return r
        k.v(v_ap)
        if xn_dram is not None:
            k.dma("gpsimd", bass.AP(xn_dram, cb * 512, [[T, 128], [128 * T, nch], [1, 512]]),
                  xwin[:, :], wait=(("sv", k.c("sv")),))
    return ("sv", k.c("sv"))


def emit_tm_to_fm(k, cm, src_tm, dst_fm, t, wait=(), nch=8, ident=None, ps=None,
                  dtile=128):
    ident = ident if ident is not None else cm.ident
    ps = ps if ps is not None else cm.pp[3]
    for ch in range(nch):
        w = wait if ch == 0 else ()
        k.t(lambda e, ch=ch: e.transpose(ps[:, 0:128],
                                         src_tm[:, ch * 128:(ch + 1) * 128], ident[:, :]),
            wait=w + (("sv", k.c("sv")),))
        k.v(lambda e, ch=ch: e.tensor_copy(dst_fm[:, ch * T + t * dtile: ch * T + t * dtile + dtile],
                                           ps[:, 0:dtile]),
            wait=(("st", k.c("st")),))
    return ("sv", k.c("sv"))


def load_w_lhsT(k, w_dram, dst_sb, kchunks, mcols, eng="gpsimd", off=0, wait=()):
    return k.dma(eng, dst_sb[:, :],
                 bass.AP(w_dram, off, [[mcols, 128], [128 * mcols, kchunks], [1, mcols]]),
                 wait=wait)



def fm_ap(dram, nch, width=T, dt_off=0):
    return bass.AP(dram, dt_off, [[width, 128], [128 * width, nch], [1, width]])


def build_l0_pre(k=None, cm=None, io=None):
    standalone = k is None
    if standalone:
        k = KB("l0pre")
        io = dict(xsh=k.din("xsh", [T, DM], F32), n1w=k.din("n1w", [DM], F32),
                  n1b=k.din("n1b", [DM], F32), wdin=k.din("wdin", [DM, 512], F32),
                  n2w=k.din("n2w", [512], F32), n2b=k.din("n2b", [512], F32),
                  xfm_o=k.dout("xfm_o", [DM, T], F32), xn_o=k.dout("xn_o", [DM, T], F32),
                  z_o=k.dout("z_o", [DM, T], BF16), xn2_a2a=k.dtmp("xn2_a2a", [512, 2048], BF16))
        cm = Common(k)
    xsh = io["xsh"]; n1w = io["n1w"]; n1b = io["n1b"]; wdin = io["wdin"]
    n2w = io["n2w"]; n2b = io["n2b"]
    xfm_o = io["xfm_o"]; xn_o = io["xn_o"]; z_o = io["z_o"]; xn2_a2a = io["xn2_a2a"]
    x_t = k.sbuf("x_t", [128, DM])
    xbuf = k.sbuf("xbuf", [128, 8 * T])
    zb = k.sbuf("zb", [128, 8 * T], BF16)
    sq = k.sbuf("sq", [128, 512])
    rows = k.sbuf("rows", [1, 3 * T])
    mb = k.sbuf("mb", [128, T])
    rb = k.sbuf("rb", [128, T])
    wr_n1 = k.sbuf("wr_n1", [128, 8])
    br_n1 = k.sbuf("br_n1", [128, 8])
    wr_n2 = k.sbuf("wr_n2", [128, 4])
    br_n2 = k.sbuf("br_n2", [128, 4])
    wd_sb = k.sbuf("wd_sb", [128, 8 * 512], BF16)
    u_fm = k.sbuf("u_fm", [128, 4 * T])
    xn2b = k.sbuf("xn2b", [128, 4 * T], BF16)

    k.dma("sync", wr_n1[:, :], bass.AP(n1w, 0, [[1, 128], [128, 8]]), allow_slow_non_contiguous=True)
    k.dma("sync", br_n1[:, :], bass.AP(n1b, 0, [[1, 128], [128, 8]]), allow_slow_non_contiguous=True)
    k.dma("sync", wr_n2[:, :], bass.AP(n2w, 0, [[1, 128], [128, 4]]), allow_slow_non_contiguous=True)
    k.dma("sync", br_n2[:, :], bass.AP(n2b, 0, [[1, 128], [128, 4]]), allow_slow_non_contiguous=True)
    load_w_lhsT(k, wdin, wd_sb, 8, 512)
    wd_ld = k.c("sg")

    for t in range(NT):
        ld = k.dma("sync", x_t[:, :], xsh[t * 128:(t + 1) * 128, :],
                   wait=(("st", k.c("st")),))
        emit_tm_to_fm(k, cm, x_t, xbuf, t, wait=(("sd", ld),))
    xfm_v = k.c("sv")
    xfm_st = k.dma("gpsimd", fm_ap(xfm_o, 8), xbuf[:, :], wait=(("sv", xfm_v),))

    emit_ln_fm(k, cm, xbuf, 8, wait=(("sv", xfm_v),),
               sq_buf=sq, rows=rows, mb=mb, rb=rb, apply_wait=(("sg", xfm_st),))
    xn_v = k.c("sv")
    xn_st = k.dma("gpsimd", fm_ap(xn_o, 8), xbuf[:, :], wait=(("sv", xn_v),))

    emit_ln_fm(k, cm, xbuf, 8, dstb=zb, wrow=wr_n1, brow=br_n1,
               wait=(("sv", xn_v),), sq_buf=sq, rows=rows, mb=mb, rb=rb,
               apply_wait=(("sg", xn_st),))
    z_v = k.c("sv")
    k.dma("gpsimd", fm_ap(z_o, 8), zb[:, :], wait=(("sv", z_v),))

    for mc in range(4):
        for cb in range(4):
            def t_mm(e, mc=mc, cb=cb):
                r = None
                for kc in range(8):
                    r = e.matmul(cm.pp[mc][:, :],
                                 wd_sb[:, kc * 512 + mc * 128: kc * 512 + mc * 128 + 128],
                                 zb[:, kc * T + cb * 512: kc * T + cb * 512 + 512],
                                 start=(kc == 0), stop=(kc == 7))
                return r
            k.t(t_mm, wait=((("sv", z_v), ("sg", wd_ld)) if (mc == 0 and cb == 0) else ()) + (("sv", k.c("sv")),))
            k.v(lambda e, mc=mc, cb=cb: e.tensor_copy(
                u_fm[:, mc * T + cb * 512: mc * T + cb * 512 + 512], cm.pp[mc][:, :]),
                wait=(("st", k.c("st")),))
    u_v = k.c("sv")

    emit_ln_fm(k, cm, u_fm, 4, dstb=xn2b, wrow=wr_n2, brow=br_n2,
               wait=(("sv", u_v),), sq_buf=sq, rows=rows, mb=mb, rb=rb)
    xn2_v = k.c("sv")
    for j in range(NCORE):
        for cl in range(4):
            k.dma("gpsimd",
                  bass.AP(xn2_a2a, j * 64 * 2048 + cl * 512, [[2048, 64], [128, 4], [1, 128]]),
                  bass.AP(xn2b.ap().tensor, (j % 2) * 64 * (4 * T) + (j // 2) * T + cl * 128,
                          [[4 * T, 64], [512, 4], [1, 128]]),
                  wait=(("sv", xn2_v),))
    k.on("gpsimd", lambda e: None, wait=(("sg", k.c("sg")),))
    return k if standalone else None


def build_l0_conv(k=None, cm=None, io=None):
    standalone = k is None
    if standalone:
        k = KB("l0conv")
        io = dict(conv_in=k.din("conv_in", [64, 128, 128], BF16),
                  crt=k.din("crt", [512, 64], F32), cit=k.din("cit", [512, 64], F32),
                  LRi=k.din("LRi", [512], F32), LIi=k.din("LIi", [512], F32),
                  Dh=k.din("Dh", [64], F32),
                  y_a2a=k.dtmp("y_a2a", [512, 2048], BF16))
        cm = Common(k)
    conv_in = io["conv_in"]; crt = io["crt"]; cit = io["cit"]
    LRi = io["LRi"]; LIi = io["LIi"]; Dh = io["Dh"]; y_a2a = io["y_a2a"]
    KREP = k.dtmp(f"KREP_{KB._uid}", [64, 128, 4352], BF16)
    zv = k.sbuf("zv", [128, 24])
    tmp = k.sbuf("ctmp", [128, 16])
    SR = k.sbuf("SR", [128, 4096])
    SI = k.sbuf("SI", [128, 4096])
    tA = k.sbuf("tA", [128, 2048])
    SRb = k.sbuf("SRb", [128, 4 * 4096], BF16)
    SIb = k.sbuf("SIb", [128, 4 * 4096], BF16)
    ccr = k.sbuf("ccr", [128, 4 * 64], BF16)
    cci = k.sbuf("cci", [128, 4 * 64], BF16)
    ccf = k.sbuf("ccf", [128, 2 * 64])
    crs = k.sbuf("crs", [128, 4 * 64])
    cis = k.sbuf("cis", [128, 4 * 64])
    Ksb = k.sbuf("Ksb", [64, 4096])
    dsb = k.sbuf("dsb", [64, 1])
    zero4k = k.sbuf("zero4k", [64, 4096])
    toep = [k.sbuf(f"toep{i}", [128, 4096], BF16) for i in range(2)]
    xin = [k.sbuf(f"xin{i}", [128, 128], BF16) for i in range(2)]
    ybf = [k.sbuf(f"ybf{i}", [128, 128], BF16) for i in range(2)]
    ytr = [k.sbuf(f"ytr{i}", [128, 128], BF16) for i in range(2)]

    k.dma("sync", crs[:, :], bass.AP(crt, 0, [[64, 128], [128 * 64, 4], [1, 64]]))
    k.dma("sync", cis[:, :], bass.AP(cit, 0, [[64, 128], [128 * 64, 4], [1, 64]]))
    k.dma("sync", zv[:, 0:4], bass.AP(LRi, 0, [[1, 128], [128, 4]]), allow_slow_non_contiguous=True)
    k.dma("sync", zv[:, 4:8], bass.AP(LIi, 0, [[1, 128], [128, 4]]), allow_slow_non_contiguous=True)
    k.dma("sync", dsb[:, 0:1], bass.AP(Dh, 0, [[1, 64], [1, 1]]))
    in_ld = k.c("sd")

    TWO_PI = 2 * math.pi
    P1C, P2C = 6.28125, TWO_PI - 6.28125
    def s_exp(e):
        e.activation(zv[:, 16:20], zv[:, 0:4], AF.Exp)
        return e.activation(zv[:, 20:24], zv[:, 4:8], AF.Exp)
    k.s(s_exp, wait=(("sd", in_ld),))
    def v_reduce(e):
        e.tensor_scalar(zv[:, 16:20], zv[:, 16:20], -1.0, None, ALU.mult)
        e.tensor_scalar(tmp[:, 0:4], zv[:, 20:24], 1.0 / TWO_PI, 8388608.0, ALU.mult, ALU.add)
        e.tensor_scalar(tmp[:, 0:4], tmp[:, 0:4], -8388608.0, None, ALU.add)
        e.tensor_scalar(tmp[:, 4:8], tmp[:, 0:4], -P1C, None, ALU.mult)
        e.tensor_tensor(tmp[:, 4:8], tmp[:, 4:8], zv[:, 20:24], ALU.add)
        e.tensor_scalar(tmp[:, 8:12], tmp[:, 0:4], -P2C, None, ALU.mult)
        e.tensor_tensor(tmp[:, 4:8], tmp[:, 4:8], tmp[:, 8:12], ALU.add)
        e.tensor_scalar(tmp[:, 12:16], zv[:, 20:24], math.pi / 2, None, ALU.add)
        e.tensor_scalar(tmp[:, 0:4], tmp[:, 12:16], 1.0 / TWO_PI, 8388608.0, ALU.mult, ALU.add)
        e.tensor_scalar(tmp[:, 0:4], tmp[:, 0:4], -8388608.0, None, ALU.add)
        e.tensor_scalar(tmp[:, 8:12], tmp[:, 0:4], -P1C, None, ALU.mult)
        e.tensor_tensor(tmp[:, 8:12], tmp[:, 8:12], zv[:, 20:24], ALU.add)
        e.tensor_scalar(tmp[:, 12:16], tmp[:, 0:4], -P2C, None, ALU.mult)
        e.tensor_tensor(tmp[:, 8:12], tmp[:, 8:12], tmp[:, 12:16], ALU.add)
        return e.tensor_scalar(tmp[:, 8:12], tmp[:, 8:12], math.pi / 2, None, ALU.add)
    k.v(v_reduce, wait=(("ss", k.c("ss")),))
    def s_sincos(e):
        e.activation(tmp[:, 4:8], tmp[:, 4:8], AF.Sin)
        e.activation(tmp[:, 8:12], tmp[:, 8:12], AF.Sin)
        return e.activation(tmp[:, 12:16], zv[:, 16:20], AF.Exp)
    k.s(s_sincos, wait=(("sv", k.c("sv")),))
    def v_z(e):
        e.tensor_tensor(zv[:, 8:12], tmp[:, 12:16], tmp[:, 8:12], ALU.mult)
        return e.tensor_tensor(zv[:, 12:16], tmp[:, 12:16], tmp[:, 4:8], ALU.mult)
    k.v(v_z, wait=(("ss", k.c("ss")),))
    def v_cc(e):
        r = None
        for nch in range(4):
            zr = zv[:, 8 + nch:9 + nch]; zi_ = zv[:, 12 + nch:13 + nch]
            lr = zv[:, 16 + nch:17 + nch]; li = zv[:, 20 + nch:21 + nch]
            e.tensor_tensor(tmp[:, 0:1], lr, lr, ALU.mult)
            e.tensor_tensor(tmp[:, 1:2], li, li, ALU.mult)
            e.tensor_tensor(tmp[:, 0:1], tmp[:, 0:1], tmp[:, 1:2], ALU.add)
            e.reciprocal(tmp[:, 0:1], tmp[:, 0:1])
            e.tensor_scalar(tmp[:, 1:2], zr, -1.0, None, ALU.add)
            e.tensor_tensor(tmp[:, 2:3], tmp[:, 1:2], lr, ALU.mult)
            e.tensor_tensor(tmp[:, 3:4], zi_, li, ALU.mult)
            e.tensor_tensor(tmp[:, 2:3], tmp[:, 2:3], tmp[:, 3:4], ALU.add)
            e.tensor_tensor(tmp[:, 2:3], tmp[:, 2:3], tmp[:, 0:1], ALU.mult)
            e.tensor_tensor(tmp[:, 3:4], zi_, lr, ALU.mult)
            e.tensor_tensor(tmp[:, 4:5], tmp[:, 1:2], li, ALU.mult)
            e.tensor_tensor(tmp[:, 3:4], tmp[:, 3:4], tmp[:, 4:5], ALU.subtract)
            e.tensor_tensor(tmp[:, 3:4], tmp[:, 3:4], tmp[:, 0:1], ALU.mult)
            e.tensor_scalar(ccf[:, 0:64], crs[:, nch * 64:(nch + 1) * 64], tmp[:, 2:3], None, ALU.mult)
            e.tensor_scalar(ccf[:, 64:128], cis[:, nch * 64:(nch + 1) * 64], tmp[:, 3:4], None, ALU.mult)
            e.tensor_tensor(ccf[:, 0:64], ccf[:, 0:64], ccf[:, 64:128], ALU.subtract)
            e.tensor_copy(ccr[:, nch * 64:(nch + 1) * 64], ccf[:, 0:64])
            e.tensor_scalar(ccf[:, 0:64], crs[:, nch * 64:(nch + 1) * 64], tmp[:, 3:4], None, ALU.mult)
            e.tensor_scalar(ccf[:, 64:128], cis[:, nch * 64:(nch + 1) * 64], tmp[:, 2:3], None, ALU.mult)
            e.tensor_tensor(ccf[:, 0:64], ccf[:, 0:64], ccf[:, 64:128], ALU.add)
            e.tensor_scalar(ccf[:, 0:64], ccf[:, 0:64], -1.0, None, ALU.mult)
            r = e.tensor_copy(cci[:, nch * 64:(nch + 1) * 64], ccf[:, 0:64])
        return r
    k.v(v_cc)

    for ncf in range(4):
        def v_schunk(e, ncf=ncf):
            e.memset(SR[:, 0:1], 1.0)
            e.memset(SI[:, 0:1], 0.0)
            e.tensor_copy(tmp[:, 0:1], zv[:, 8 + ncf:9 + ncf])
            e.tensor_copy(tmp[:, 1:2], zv[:, 12 + ncf:13 + ncf])
            m = 1
            while m < 4096:
                e.tensor_scalar(SR[:, m:2 * m], SI[:, 0:m], tmp[:, 1:2], None, ALU.mult)
                e.tensor_scalar(tA[:, 0:m], SR[:, 0:m], tmp[:, 0:1], None, ALU.mult)
                e.tensor_tensor(SR[:, m:2 * m], tA[:, 0:m], SR[:, m:2 * m], ALU.subtract)
                e.tensor_scalar(SI[:, m:2 * m], SI[:, 0:m], tmp[:, 0:1], None, ALU.mult)
                e.tensor_scalar(tA[:, 0:m], SR[:, 0:m], tmp[:, 1:2], None, ALU.mult)
                e.tensor_tensor(SI[:, m:2 * m], tA[:, 0:m], SI[:, m:2 * m], ALU.add)
                e.tensor_tensor(tmp[:, 2:3], tmp[:, 0:1], tmp[:, 0:1], ALU.mult)
                e.tensor_tensor(tmp[:, 3:4], tmp[:, 1:2], tmp[:, 1:2], ALU.mult)
                e.tensor_tensor(tmp[:, 4:5], tmp[:, 0:1], tmp[:, 1:2], ALU.mult)
                e.tensor_tensor(tmp[:, 0:1], tmp[:, 2:3], tmp[:, 3:4], ALU.subtract)
                e.tensor_scalar(tmp[:, 1:2], tmp[:, 4:5], 2.0, None, ALU.mult)
                m *= 2
            e.tensor_copy(SRb[:, ncf * 4096:(ncf + 1) * 4096], SR[:, :])
            return e.tensor_copy(SIb[:, ncf * 4096:(ncf + 1) * 4096], SI[:, :])
        k.v(v_schunk)
    s_v = k.c("sv")
    for lb in range(8):
        def t_k(e, lb=lb):
            r = None
            for ncf in range(4):
                e.matmul(cm.pp[0][0:64, :], ccr[:, ncf * 64:(ncf + 1) * 64],
                         SRb[:, ncf * 4096 + lb * 512: ncf * 4096 + lb * 512 + 512],
                         start=(ncf == 0), stop=False)
                r = e.matmul(cm.pp[0][0:64, :], cci[:, ncf * 64:(ncf + 1) * 64],
                             SIb[:, ncf * 4096 + lb * 512: ncf * 4096 + lb * 512 + 512],
                             start=False, stop=(ncf == 3))
            return r
        k.t(t_k, wait=(("sv", s_v),) if lb == 0 else (("sv", k.c("sv")),))
        k.v(lambda e, lb=lb: e.tensor_copy(Ksb[:, lb * 512:(lb + 1) * 512], cm.pp[0][0:64, :]),
            wait=(("st", k.c("st")),))
    def v_dfold(e):
        e.memset(zero4k[:, :], 0.0)
        return e.tensor_tensor(Ksb[:, 0:1], Ksb[:, 0:1], dsb[:, 0:1], ALU.add)
    k.v(v_dfold)
    k_v = k.c("sv")

    for g in range(8):
        k.dma("gpsimd", bass.AP(KREP, g * 16 * 4352, [[128 * 4352, 64], [4352, 16], [1, 256]]),
              zero4k[:, 0:4096], wait=(("sv", k_v),) if g == 0 else ())
    zd = k.c("sg")
    for j in range(128):
        k.dma("gpsimd", bass.AP(KREP, j * 4352 + 127 + j, [[128 * 4352, 64], [1, 4096]]),
              Ksb[:, 0:4096], wait=(("sg", zd),) if j == 0 else ())
    krep_d = k.c("sg")

    comp_done = {}
    out_dma = {}
    for hl in range(64):
        bi = hl % 2
        w = (("sg", krep_d),) if hl < 2 else (("st", comp_done[hl - 2]),)
        tl = k.dma("sync", toep[bi][:, :],
                   bass.AP(KREP, hl * 128 * 4352 + 127, [[4352, 128], [128, 32], [1, 128]]),
                   wait=w)
        xl = k.dma("sync", xin[bi][:, :], conv_in[hl, :, :], transpose=True)
        def t_conv(e, hl=hl, bi=bi, tl=tl, xl=xl):
            e.wait_ge(k._sem["sd"], max(tl, xl))
            r = None
            for dl in range(32):
                if dl == 0:
                    r = e.matmul(cm.pp[bi][:, 0:128], toep[bi][:, 0:128],
                                 xin[bi][:, 0:128], start=True, stop=False)
                else:
                    r = e.matmul(cm.pp[bi][:, 4 * dl:128],
                                 toep[bi][:, dl * 128:dl * 128 + 128],
                                 xin[bi][:, 0:128 - 4 * dl], start=False, stop=(dl == 31))
            return r
        k.t(t_conv, wait=(("sv", k.c("sv")),) if hl >= 2 else ())
        k.v(lambda e, bi=bi: e.tensor_copy(ybf[bi][:, :], cm.pp[bi][:, 0:128]),
            wait=(("st", k.c("st")),))
        def t_tr(e, bi=bi):
            return e.transpose(cm.pbf[:, bi * 128:bi * 128 + 128], ybf[bi][:, :], cm.identb[:, :])
        k.t(t_tr, wait=(("sv", k.c("sv")),))
        comp_done[hl] = k.c("st")
        wdma = (("sg", out_dma[hl - 2]),) if hl >= 2 else ()
        k.v(lambda e, bi=bi: e.tensor_copy(ytr[bi][:, :], cm.pbf[:, bi * 128:bi * 128 + 128]),
            wait=(("st", k.c("st")),) + wdma)
        k.dma("gpsimd",
              bass.AP(y_a2a, hl * 16 * 128, [[64 * 16 * 128, 8], [128, 16], [1, 128]]),
              ytr[bi][:, :], wait=(("sv", k.c("sv")),))
        out_dma[hl] = k.c("sg")
    k.on("gpsimd", lambda e: None, wait=(("sg", k.c("sg")), ("sd", k.c("sd"))))
    return k if standalone else None


def build_l0_post(k=None, cm=None, io=None):
    standalone = k is None
    if standalone:
        k = KB("l0post")
        io = dict(xfm_i=k.din("xfm_i", [DM, T], F32), xn_i=k.din("xn_i", [DM, T], F32),
                  z_i=k.din("z_i", [DM, T], BF16), d_a2a=k.dtmp("d_a2a", [512, 2048], BF16),
                  Wxg=k.din("Wxg", [DM, 4096], F32), Wgate=k.din("Wgate", [512, 4096], F32),
                  Wout=k.din("Wout", [4096, DM], F32), bout=k.din("bout", [DM], F32),
                  gate_w=k.din("gate_w", [DM, 8], F32), gate_b=k.din("gate_b", [8], F32),
                  w1=k.din("w1", [8, DM, 4096], F32), b1=k.din("b1", [8, 4096], F32),
                  w2=k.din("w2", [8, 4096, DM], F32), b2=k.din("b2", [8, DM], F32),
                  R1_o=k.dout("R1_o", [DM, T], F32))
        cm = Common(k)
    xfm_i = io["xfm_i"]; xn_i = io["xn_i"]; z_i = io["z_i"]; d_a2a = io["d_a2a"]
    Wxg = io["Wxg"]; Wgate = io["Wgate"]; Wout = io["Wout"]; bout = io["bout"]
    gate_w = io["gate_w"]; gate_b = io["gate_b"]
    w1 = io["w1"]; b1 = io["b1"]; w2 = io["w2"]; b2 = io["b2"]
    R1_o = io["R1_o"]
    prod_d = k.dtmp(f"prod_d_{KB._uid}", [8, 4096, 256], BF16)
    wbufA = k.sbuf("wbufA", [128, 32768], BF16)
    wbufB = k.sbuf("wbufB", [128, 32768], BF16)
    zxn = k.sbuf("zxn", [128, 8 * T], BF16)
    pbuf = k.sbuf("pbuf", [128, 8192], BF16)
    outf = k.sbuf("outf", [128, 1024])
    xgb = k.sbuf("xgb", [128, 256], BF16)
    bout_sb = k.sbuf("bout_sb", [128, 8])
    b1_sb = k.sbuf("b1_sb", [128, 32])
    b2_sb = k.sbuf("b2_sb", [8, DM])
    gw_sb = k.sbuf("gw_sb", [128, 64], BF16)
    gb_sb = k.sbuf("gb_sb", [8, 1])
    glog = k.sbuf("glog", [8, T])
    ce_rows = k.sbuf("ce_rows", [8, T])
    ceb = k.sbuf("ceb", [128, T], BF16)
    ptm = k.sbuf("ptm", [128, 8])
    stat = k.sbuf("stat", [128, 8])
    neginf = k.sbuf("neginf", [128, 8])
    d_fm = wbufB

    r1b = k.dma("gpsimd", fm_ap(R1_o, 8), fm_ap(xfm_i, 8))
    r1b = k.dma("gpsimd", fm_ap(R1_o, 8), fm_ap(xn_i, 8), wait=(("sg", r1b),),
                accum_op=ALU.add)
    zl = k.dma("gpsimd", zxn[:, :], fm_ap(z_i, 8))
    dl = 0
    for j in range(NCORE):
        for bb in range(4):
            dl = k.dma("gpsimd",
                       bass.AP(wbufB.ap().tensor, (j % 2) * 64 * 32768 + (j // 2) * T + bb * 512,
                               [[32768, 64], [128, 4], [1, 128]]),
                       bass.AP(d_a2a, j * 64 * 2048 + bb * 128, [[2048, 64], [512, 4], [1, 128]]))
    wxgl = load_w_lhsT(k, Wxg, wbufA, 8, 4096)
    def wg_load(e):
        return e.dma_start(out=wbufB[:, 16384:32768],
                           in_=bass.AP(Wgate, 0, [[4096, 128], [128 * 4096, 4], [1, 4096]]))
    k.on("gpsimd", wg_load, inc=(("sg", 16),))
    wgl = k.c("sg")
    k.dma("sync", bout_sb[:, :], bass.AP(bout, 0, [[1, 128], [128, 8]]), allow_slow_non_contiguous=True)
    k.dma("sync", gb_sb[0:8, 0:1], bass.AP(gate_b, 0, [[1, 8], [1, 1]]))
    k.dma("sync", b2_sb[0:8, :], b2[:, :])
    k.dma("gpsimd", gw_sb[:, :], bass.AP(gate_w, 0, [[8, 128], [128 * 8, 8], [1, 8]]))
    small_ld = k.c("sd"); gwl = k.c("sg")

    prod_store = 0
    for t in range(8):
        for mc in range(32):
            def t_xg(e, t=t, mc=mc):
                r = None
                for kc in range(8):
                    r = e.matmul(cm.pp[0][:, 0:256],
                                 wbufA[:, kc * 4096 + mc * 128: kc * 4096 + mc * 128 + 128],
                                 zxn[:, kc * T + t * 256: kc * T + t * 256 + 256],
                                 start=(kc == 0), stop=(kc == 7))
                for kc in range(4):
                    r = e.matmul(cm.pp[1][:, 0:256],
                                 wbufB[:, 16384 + kc * 4096 + mc * 128: 16384 + kc * 4096 + mc * 128 + 128],
                                 wbufB[:, kc * T + t * 256: kc * T + t * 256 + 256],
                                 start=(kc == 0), stop=(kc == 3))
                return r
            w0 = (("sg", max(zl, dl, wxgl, wgl)),) if (t == 0 and mc == 0) else ()
            k.t(t_xg, wait=w0 + (("sv", k.c("sv")), ("ss", k.c("ss"))))
            k.s(lambda e: e.activation(xgb[:, :], cm.pp[0][:, 0:256], AF.Copy),
                wait=(("st", k.c("st")),))
            def v_prod(e, mc=mc):
                e.tensor_copy(pbuf[:, mc * 256:(mc + 1) * 256], cm.pp[1][:, 0:256])
                return e.tensor_tensor(pbuf[:, mc * 256:(mc + 1) * 256],
                                       pbuf[:, mc * 256:(mc + 1) * 256], xgb[:, :], ALU.mult)
            wps = (("sg", prod_store),) if (mc == 0 and t > 0) else ()
            k.v(v_prod, wait=(("st", k.c("st")), ("ss", k.c("ss"))) + wps)
        k.dma("gpsimd", bass.AP(prod_d, t * 4096 * 256, [[256, 128], [128 * 256, 32], [1, 256]]),
              pbuf[:, :], wait=(("sv", k.c("sv")),))
        prod_store = k.c("sg")
    pass1_done_v = k.c("sv"); pass1_st = k.c("st"); prod_wr = k.c("sg")

    woutl = k.dma("gpsimd", wbufB[:, :],
                  bass.AP(Wout, 0, [[DM, 128], [128 * DM, 32], [1, DM]]),
                  wait=(("st", pass1_st),))
    xnl = k.dma("gpsimd", zxn[:, :], fm_ap(xn_i, 8), wait=(("st", pass1_st),))

    for cb in range(4):
        def t_g(e, cb=cb):
            r = None
            for kc in range(8):
                r = e.matmul(cm.pp[0][0:8, :], gw_sb[:, kc * 8:(kc + 1) * 8],
                             zxn[:, kc * T + cb * 512: kc * T + cb * 512 + 512],
                             start=(kc == 0), stop=(kc == 7))
            return r
        k.t(t_g, wait=(("sg", xnl), ("sv", k.c("sv"))) if cb == 0 else (("sv", k.c("sv")),))
        k.v(lambda e, cb=cb: e.tensor_scalar(glog[0:8, cb * 512:(cb + 1) * 512],
                                             cm.pp[0][0:8, :], gb_sb[0:8, 0:1], None, ALU.add),
            wait=(("st", k.c("st")), ("sd", small_ld)))
    glog_s = k.c("sv")
    k.v(lambda e: e.memset(neginf[:, :], -1e30), wait=())
    for t in range(NT):
        def t_tr(e, t=t):
            return e.transpose(cm.pp[1][:, 0:8], glog[0:8, t * 128:(t + 1) * 128],
                               cm.ident[0:8, 0:8])
        k.t(t_tr, wait=(("sv", glog_s),) if t == 0 else (("sv", k.c("sv")),))
        def v_ce(e, t=t):
            e.tensor_copy(ptm[:, :], cm.pp[1][:, 0:8])
            e.tensor_reduce(stat[:, 0:1], ptm[:, :], axis=AX, op=ALU.max)
            e.tensor_scalar(stat[:, 1:2], stat[:, 0:1], -1.0, None, ALU.mult)
            return stat
        k.v(v_ce, wait=(("st", k.c("st")),))
        k.s(lambda e: e.activation(ptm[:, :], ptm[:, :], AF.Exp, bias=stat[:, 1:2]),
            wait=(("sv", k.c("sv")),))
        def v_sm(e, t=t):
            e.tensor_reduce(stat[:, 2:3], ptm[:, :], axis=AX, op=ALU.add)
            e.reciprocal(stat[:, 2:3], stat[:, 2:3])
            e.tensor_scalar(ptm[:, :], ptm[:, :], stat[:, 2:3], None, ALU.mult)
            e.tensor_reduce(stat[:, 3:4], ptm[:, :], axis=AX, op=ALU.max)
            e.tensor_scalar(outf[:, 0:8], ptm[:, :], stat[:, 3:4], None, ALU.is_ge)
            e.tensor_scalar(outf[:, 8:16], outf[:, 0:8], -1e30, None, ALU.mult)
            e.tensor_tensor(outf[:, 8:16], ptm[:, :], outf[:, 8:16], ALU.add)
            e.tensor_reduce(stat[:, 5:6], outf[:, 8:16], axis=AX, op=ALU.max)
            e.tensor_scalar(outf[:, 0:8], ptm[:, :], stat[:, 5:6], None, ALU.is_ge)
            e.tensor_tensor(stat[:, 6:7], stat[:, 3:4], stat[:, 5:6], ALU.add)
            e.reciprocal(stat[:, 6:7], stat[:, 6:7])
            e.tensor_tensor(ptm[:, :], ptm[:, :], outf[:, 0:8], ALU.mult)
            return e.tensor_scalar(ptm[:, :], ptm[:, :], stat[:, 6:7], None, ALU.mult)
        k.v(v_sm, wait=(("ss", k.c("ss")),))
        def t_tr2(e, t=t):
            return e.transpose(cm.pp[1][0:8, 0:128], ptm[:, :], cm.ident[:, :])
        k.t(t_tr2, wait=(("sv", k.c("sv")),))
        k.v(lambda e, t=t: e.tensor_copy(ce_rows[0:8, t * 128:(t + 1) * 128],
                                         cm.pp[1][0:8, 0:128]),
            wait=(("st", k.c("st")),))
    ce_v = k.c("sv")

    attn_acc = {}
    for t in range(8):
        pl = k.dma("sync", pbuf[:, :],
                   bass.AP(prod_d, t * 4096 * 256, [[256, 128], [128 * 256, 32], [1, 256]]),
                   wait=(("sg", prod_wr), ("st", k.c("st"))))
        def t_out(e, t=t, pl=pl):
            e.wait_ge(k._sem["sd"], pl)
            r = None
            for oc in range(8):
                ps = cm.pp[2 + oc // 2][:, (oc % 2) * 256:(oc % 2) * 256 + 256]
                for mc in range(32):
                    r = e.matmul(ps, wbufB[:, mc * DM + oc * 128: mc * DM + oc * 128 + 128],
                                 pbuf[:, mc * 256:(mc + 1) * 256],
                                 start=(mc == 0), stop=(mc == 31))
            return r
        k.t(t_out, wait=(("sg", woutl), ("ss", k.c("ss")), ("sv", k.c("sv"))))
        for half in range(2):
            def s_ev(e, t=t, half=half):
                r = None
                for oc in range(half * 4, half * 4 + 4):
                    ps = cm.pp[2 + oc // 2][:, (oc % 2) * 256:(oc % 2) * 256 + 256]
                    r = e.tensor_scalar(outf[:, (oc % 4) * 256:(oc % 4) * 256 + 256], ps,
                                        bout_sb[:, oc:oc + 1], None, ALU.add)
                return r
            k.v(s_ev, wait=(("st", k.c("st")), ("sg", k.c("sg"))))
            k.dma("gpsimd", bass.AP(R1_o, half * 4 * 128 * T + t * 256, [[T, 128], [128 * T, 4], [1, 256]]),
                  outf[:, :], wait=(("sv", k.c("sv")), ("sg", r1b)), accum_op=ALU.add)
        attn_acc[t] = k.c("sg")
    attn_acc_all = k.c("sg")

    ffn_acc = {}
    for e_i in range(8):
        w1l = k.dma("gpsimd", wbufA[:, :],
                    bass.AP(w1, e_i * DM * 4096, [[4096, 128], [128 * 4096, 8], [1, 4096]]),
                    wait=(("st", k.c("st")),) if e_i > 0 else (("st", pass1_st),))
        w2l = k.dma("gpsimd", wbufB[:, :],
                    bass.AP(w2, e_i * 4096 * DM, [[DM, 128], [128 * DM, 32], [1, DM]]),
                    wait=(("st", k.c("st")),))
        b1l = k.dma("sync", b1_sb[:, :], bass.AP(b1, e_i * 4096, [[1, 128], [128, 32]]),
                    allow_slow_non_contiguous=True, wait=(("ss", k.c("ss")),))
        ce_stage = glog
        cel = k.dma("sync", ce_stage[0:1, :], ce_rows[e_i:e_i + 1, :],
                    wait=(("sv", max(ce_v, k.c("sv"))), ("st", k.c("st"))))
        for cb in range(4):
            def t_bc(e, cb=cb, cel=cel):
                e.wait_ge(k._sem["sd"], cel)
                return e.matmul(cm.pp[1][:, :], cm.ones_row[0:1, :],
                                ce_stage[0:1, cb * 512:(cb + 1) * 512],
                                start=True, stop=True)
            k.t(t_bc, wait=(("sv", k.c("sv")),))
            k.v(lambda e, cb=cb: e.tensor_copy(ceb[:, cb * 512:(cb + 1) * 512], cm.pp[1][:, :]),
                wait=(("st", k.c("st")),))
        ceb_v = k.c("sv")
        for t in range(8):
            for mc in range(32):
                def t_h(e, e_i=e_i, t=t, mc=mc, w1l=w1l):
                    if t == 0 and mc == 0:
                        e.wait_ge(k._sem["sg"], w1l)
                    return [e.matmul(cm.pp[0][:, 0:256],
                                     wbufA[:, kc * 4096 + mc * 128: kc * 4096 + mc * 128 + 128],
                                     zxn[:, kc * T + t * 256: kc * T + t * 256 + 256],
                                     start=(kc == 0), stop=(kc == 7))
                            for kc in range(8)][-1]
                k.t(t_h, wait=(("sv", k.c("sv")), ("ss", k.c("ss"))))
                k.s(lambda e, mc=mc: e.activation(pbuf[:, mc * 256:(mc + 1) * 256],
                                                  cm.pp[0][:, 0:256], AF.Relu,
                                                  bias=b1_sb[:, mc:mc + 1]),
                    wait=(("st", k.c("st")), ("sd", b1l)))
                k.v(lambda e, mc=mc, t=t: e.tensor_tensor(
                    pbuf[:, mc * 256:(mc + 1) * 256], pbuf[:, mc * 256:(mc + 1) * 256],
                    ceb[:, t * 256:(t + 1) * 256], ALU.mult),
                    wait=(("ss", k.c("ss")), ("sv", ceb_v)))
            h_v = k.c("sv")
            def t_o(e, w2l=w2l, h_v=h_v):
                e.wait_ge(k._sem["sg"], w2l)
                r = None
                for oc in range(8):
                    ps = cm.pp[2 + oc // 2][:, (oc % 2) * 256:(oc % 2) * 256 + 256]
                    for mc in range(32):
                        r = e.matmul(ps, wbufB[:, mc * DM + oc * 128: mc * DM + oc * 128 + 128],
                                     pbuf[:, mc * 256:(mc + 1) * 256],
                                     start=(mc == 0), stop=(mc == 31))
                return r
            k.t(t_o, wait=(("sv", h_v), ("ss", k.c("ss"))))
            prev = ffn_acc.get((e_i - 1, t), attn_acc_all)
            for half in range(2):
                def s_ev2(e, half=half):
                    r = None
                    for oc in range(half * 4, half * 4 + 4):
                        ps = cm.pp[2 + oc // 2][:, (oc % 2) * 256:(oc % 2) * 256 + 256]
                        r = e.activation(outf[:, (oc % 4) * 256:(oc % 4) * 256 + 256], ps, AF.Copy)
                    return r
                k.s(s_ev2, wait=(("st", k.c("st")), ("sg", k.c("sg"))))
                k.dma("gpsimd", bass.AP(R1_o, half * 4 * 128 * T + t * 256, [[T, 128], [128 * T, 4], [1, 256]]),
                      outf[:, :], wait=(("ss", k.c("ss")), ("sg", prev)), accum_op=ALU.add)
            ffn_acc[(e_i, t)] = k.c("sg")
    ffn_all = k.c("sg")

    prev_acc = ffn_all
    for cb in range(4):
        for oc in range(8):
            def t_b2(e, cb=cb, oc=oc):
                return e.matmul(cm.pp[0][:, :], b2_sb[0:8, oc * 128:(oc + 1) * 128],
                                ce_rows[0:8, cb * 512:(cb + 1) * 512],
                                start=True, stop=True)
            k.t(t_b2, wait=(("sv", k.c("sv")),) + ((("sg", ffn_all),) if (cb == 0 and oc == 0) else ()))
            k.v(lambda e: e.tensor_copy(outf[:, 0:512], cm.pp[0][:, :]),
                wait=(("st", k.c("st")), ("sg", k.c("sg"))))
            k.dma("gpsimd", bass.AP(R1_o, oc * 128 * T + cb * 512, [[T, 128], [1, 512]]),
                  outf[:, 0:512], wait=(("sv", k.c("sv")), ("sg", prev_acc)),
                  accum_op=ALU.add)
            prev_acc = k.c("sg")
    k.on("gpsimd", lambda e: None, wait=(("sg", k.c("sg")), ("sd", k.c("sd"))))
    return k


def build_l1_attn(k=None, cm=None, io=None):
    standalone = k is None
    if standalone:
        k = KB("l1attn")
        io = dict(R1_i=k.din("R1_i", [DM, T], F32), wqT=k.din("wqT", [DM, DM], F32),
                  wkT=k.din("wkT", [DM, DM], F32), wvT=k.din("wvT", [DM, DM], F32),
                  wrT=k.din("wrT", [DM, DM], F32), cosI=k.din("cosI", [64, 4096], F32),
                  sinI=k.din("sinI", [64, 4096], F32), p64T=k.din("p64T", [64, 64], F32),
                  xn_o=k.dout("xn_o", [DM, T], F32),
                  x3_o=k.dout("x3_o", [8, 4, 128, 512], F32))
        cm = Common(k)
    R1_i = io["R1_i"]; wqT = io["wqT"]; wkT = io["wkT"]; wvT = io["wvT"]; wrT = io["wrT"]
    cosI = io["cosI"]; sinI = io["sinI"]; p64T = io["p64T"]
    xn_o = io["xn_o"]; x3_o = io["x3_o"]
    v_dram = k.dtmp(f"v_dram_{KB._uid}", [8, 256, DM], BF16)
    xwin = k.sbuf("xwin", [128, 8 * 512])
    xnb = k.sbuf("xnb", [128, 8 * T], BF16)
    sq = k.sbuf("sq", [128, 512])
    rows = k.sbuf("rows", [1, 2 * T])
    mb = k.sbuf("mb", [128, T])
    rb = k.sbuf("rb", [128, T])
    wq_sb = k.sbuf("wq_sb", [128, 8 * DM], BF16)
    wk_sb = k.sbuf("wk_sb", [128, 8 * DM], BF16)
    wv_sb = k.sbuf("wv_sb", [128, 8 * DM], BF16)
    wr_sb = k.sbuf("wr_sb", [128, 8 * DM], BF16)
    cosb = k.sbuf("cosb", [64, 4096], BF16)
    sinb = k.sbuf("sinb", [64, 4096], BF16)
    p64b = k.sbuf("p64b", [64, 64], BF16)
    vsl = k.sbuf("vsl", [128, DM], BF16)
    QT = k.sbuf("QT", [64, 4096], BF16)
    KT = k.sbuf("KT", [64, 4096], BF16)
    rt1 = k.sbuf("rt1", [64, 512], BF16)
    rt2 = k.sbuf("rt2", [64, 512], BF16)
    PTb = [k.sbuf(f"PTb{i}", [128, 512], BF16) for i in range(2)]
    Vr = k.sbuf("Vr", [128, 32 * 65], BF16)
    out_cm = k.sbuf("out_cm", [128, 4096], BF16)
    rinv = rows
    rb64 = mb
    x2T = k.sbuf("x2T", [128, 8 * 512], BF16)
    xout = k.sbuf("xout", [128, 512])

    load_w_lhsT(k, wqT, wq_sb, 8, DM)
    load_w_lhsT(k, wkT, wk_sb, 8, DM)
    load_w_lhsT(k, wvT, wv_sb, 8, DM)
    load_w_lhsT(k, wrT, wr_sb, 8, DM)
    wl = k.c("sg")
    k.dma("gpsimd", cosb[:, :], cosI[:, :])
    k.dma("gpsimd", sinb[:, :], sinI[:, :])
    k.dma("gpsimd", p64b[:, :], p64T[:, :])
    tl = k.c("sg")

    emit_ln_stream(k, cm, R1_i, 8, xwin, xnb, xn_dram=xn_o,
                   sq_buf=sq, rows=rows, mb=mb, rb=rb)
    xn_v = k.c("sv")

    for b in range(4):
        for hh in range(2):
            pair = b * 2 + hh
            tok0 = b * 512 + hh * 256
            for mc in range(8):
                def t_qk1(e, tok0=tok0, mc=mc):
                    r = None
                    for kc in range(8):
                        e.matmul(cm.pp[3][:, 0:256],
                                 wq_sb[:, kc * DM + mc * 128: kc * DM + mc * 128 + 128],
                                 xnb[:, kc * T + tok0: kc * T + tok0 + 256],
                                 start=(kc == 0), stop=(kc == 7))
                    for kc in range(8):
                        r = e.matmul(cm.pp[4][:, 0:256],
                                     wk_sb[:, kc * DM + mc * 128: kc * DM + mc * 128 + 128],
                                     xnb[:, kc * T + tok0: kc * T + tok0 + 256],
                                     start=(kc == 0), stop=(kc == 7))
                    return r
                w0 = (("sv", xn_v), ("sg", wl)) if (pair == 0 and mc == 0) else ()
                k.t(t_qk1, wait=w0 + (("sv", k.c("sv")),))
                def v_ev(e, mc=mc):
                    r = None
                    for half in range(2):
                        ob = 2 * mc + half
                        r = e.tensor_copy(bass.AP(QT.ap().tensor, ob, [[4096, 64], [16, 256]]),
                                          cm.pp[3][half * 64:half * 64 + 64, 0:256])
                        r = e.tensor_copy(bass.AP(KT.ap().tensor, ob, [[4096, 64], [16, 256]]),
                                          cm.pp[4][half * 64:half * 64 + 64, 0:256])
                    return r
                k.v(v_ev, wait=(("st", k.c("st")),))
            vw = 0
            for tt in range(2):
                for nb in range(2):
                    def t_v(e, tok0=tok0, tt=tt, nb=nb):
                        r = None
                        for kc in range(8):
                            r = e.matmul(cm.pp[3][:, :],
                                         xnb[:, kc * T + tok0 + tt * 128: kc * T + tok0 + tt * 128 + 128],
                                         wv_sb[:, kc * DM + nb * 512: kc * DM + nb * 512 + 512],
                                         start=(kc == 0), stop=(kc == 7))
                        return r
                    k.t(t_v, wait=(("sv", k.c("sv")), ("sg", vw)))
                    k.v(lambda e, nb=nb: e.tensor_copy(
                        vsl[:, nb * 512: nb * 512 + 512], cm.pp[3][:, :]),
                        wait=(("st", k.c("st")),))
                vw = k.dma("gpsimd", bass.AP(v_dram, pair * 256 * DM + tt * 128 * DM, [[DM, 128], [1, DM]]),
                           vsl[:, :], wait=(("sv", k.c("sv")),))
            def g_vr(e, pair=pair, vw=vw):
                e.wait_ge(k._sem["sg"], vw)
                return e.dma_start(
                    out=bass.AP(Vr.ap().tensor, 0, [[32 * 65, 128], [65, 32], [1, 64]]),
                    in_=bass.AP(v_dram, pair * 256 * DM,
                                [[DM, 8], [64, 16], [8 * DM, 32], [1, 64]]))
            k.on("gpsimd", g_vr, wait=(("st", k.c("st")),), inc=(("sg", 16),))
            vr_l = k.c("sg")
            k.v(lambda e: e.memset(bass.AP(Vr.ap().tensor, 64, [[32 * 65, 128], [65, 32], [1, 1]]), 1.0),
                wait=(("sg", vr_l),))
            vr_v = k.c("sv")
            for TQ in (QT, KT):
                for nb in range(8):
                    def t_rot(e, TQ=TQ, nb=nb):
                        return e.matmul(cm.pp[3][0:64, :], p64b[:, :],
                                        TQ[:, nb * 512:(nb + 1) * 512], start=True, stop=True)
                    k.t(t_rot, wait=(("sv", k.c("sv")), ("sg", tl)))
                    def v_rope(e, TQ=TQ, nb=nb):
                        e.tensor_tensor(rt1[:, :], TQ[:, nb * 512:(nb + 1) * 512],
                                        cosb[:, nb * 512:(nb + 1) * 512], ALU.mult)
                        e.tensor_tensor(rt2[:, :], cm.pp[3][0:64, :],
                                        sinb[:, nb * 512:(nb + 1) * 512], ALU.mult)
                        return e.tensor_tensor(TQ[:, nb * 512:(nb + 1) * 512],
                                               rt1[:, :], rt2[:, :], ALU.add)
                    k.v(v_rope, wait=(("st", k.c("st")),))
            rope_v = k.c("sv")
            for qb in range(8):
                for kc in range(32):
                    def t_s(e, qb=qb, kc=kc):
                        return e.matmul(cm.pp[0][:, :], KT[:, kc * 128:(kc + 1) * 128],
                                        QT[:, qb * 512:(qb + 1) * 512], start=True, stop=True)
                    w0 = (("sv", rope_v),) if (qb == 0 and kc == 0) else ()
                    if kc == 0:
                        w0 = w0 + (("sv", k.c("sv")),)
                    k.t(t_s, wait=w0 + (("ss", k.c("ss")),))
                    k.s(lambda e, kc=kc: e.activation(PTb[kc % 2][:, :], cm.pp[0][:, :],
                                                      AF.Exp, scale=0.25),
                        wait=(("st", k.c("st")),))
                    def t_pv(e, kc=kc):
                        return e.matmul(cm.pp[1][0:65, :], Vr[:, kc * 65:(kc + 1) * 65],
                                        PTb[kc % 2][:, :], start=(kc == 0), stop=(kc == 31))
                    k.t(t_pv, wait=(("ss", k.c("ss")),))
                k.v(lambda e: e.reciprocal(rinv[0:1, 0:512], cm.pp[1][64:65, :]),
                    wait=(("st", k.c("st")),))
                def t_bc(e):
                    return e.matmul(cm.pp[2][0:64, :], cm.ones_row[0:1, 0:64],
                                    rinv[0:1, 0:512], start=True, stop=True)
                k.t(t_bc, wait=(("sv", k.c("sv")),))
                k.v(lambda e: e.tensor_copy(rb64[0:64, 0:512], cm.pp[2][0:64, :]),
                    wait=(("st", k.c("st")),))
                def v_norm(e, b=b, hh=hh, qb=qb):
                    return e.tensor_tensor(
                        out_cm[hh * 64:hh * 64 + 64, qb * 512: qb * 512 + 512],
                        cm.pp[1][0:64, :], rb64[0:64, 0:512], ALU.mult)
                k.v(v_norm, wait=(("sv", k.c("sv")),))
        if hh != 1:
            continue
        for a in range(4):
            for qch in range(8):
                s2chunk = a * 8 + qch
                def t_tr(e, b=b, s2chunk=s2chunk):
                    return e.transpose(cm.pbf[:, 0:128],
                                       out_cm[:, s2chunk * 128: s2chunk * 128 + 128],
                                       cm.identb[:, :])
                k.t(t_tr, wait=(("sv", k.c("sv")),))
                def v_tr(e, a=a, qch=qch):
                    return e.tensor_copy(bass.AP(x2T.ap().tensor, qch * 512 + a, [[8 * 512, 128], [4, 128]]),
                                         cm.pbf[:, 0:128])
                k.v(v_tr, wait=(("st", k.c("st")),))
        x2t_v = k.c("sv")
        for a in range(4):
            for ob in range(2):
                def t_x3(e, a=a, ob=ob, b=b):
                    r = None
                    for cc in range(8):
                        r = e.matmul(cm.pp[3][:, :],
                                     bass.AP(x2T.ap().tensor, cc * 512 + a, [[8 * 512, 128], [4, 128]]),
                                     wr_sb[:, cc * DM + ob * 512: cc * DM + ob * 512 + 512],
                                     start=(cc == 0), stop=(cc == 7))
                    return r
                k.t(t_x3, wait=(("sv", max(x2t_v, k.c("sv"))),))
                k.v(lambda e: e.tensor_copy(xout[:, :], cm.pp[3][:, :]),
                    wait=(("st", k.c("st")),))
                d = a * 2 + ob
                k.dma("gpsimd", bass.AP(x3_o, d * 4 * 512 * 128 + b * 512 * 128, [[512, 128], [1, 512]]),
                      xout[:, :], wait=(("sv", k.c("sv")),))
    k.on("gpsimd", lambda e: None, wait=(("sg", k.c("sg")), ("sd", k.c("sd"))))
    return k if standalone else None


def build_l1_ffn(k=None, cm=None, io=None, init_out=True, out_gate=0):
    standalone = k is None
    if standalone:
        k = KB("l1ffn")
        io = dict(xn_i=k.din("xn_i", [DM, T], F32), gwT=k.din("gwT", [DM, 16], F32),
                  w1=k.din("w1", [16, DM, 512], F32), w2=k.din("w2", [16, 512, DM], F32),
                  w3=k.din("w3", [16, DM, 512], F32), ws1=k.din("ws1", [DM, DM], F32),
                  ws2=k.din("ws2", [DM, DM], F32), ws3=k.din("ws3", [DM, DM], F32),
                  f1_o=k.dout("f1_o", [DM, T], F32))
        cm = Common(k)
    xn_i = io["xn_i"]; gwT = io["gwT"]
    w1 = io["w1"]; w2 = io["w2"]; w3 = io["w3"]
    ws1 = io["ws1"]; ws2 = io["ws2"]; ws3 = io["ws3"]
    f1_o = io["f1_o"]
    xnb = k.sbuf("xnb", [128, 8 * T], BF16)
    wA = k.sbuf("wA", [128, 8 * DM], BF16)
    wB = k.sbuf("wB", [128, 8 * DM], BF16)
    wC = k.sbuf("wC", [128, 8 * DM], BF16)
    gw_sb = k.sbuf("gw_sb", [128, 8 * 16], BF16)
    glog = k.sbuf("glog", [16, T])
    ce_rows = k.sbuf("ce_rows", [16, T])
    ceb = k.sbuf("ceb", [128, T], BF16)
    ptm = k.sbuf("ptm", [128, 16])
    gtm = k.sbuf("gtm", [128, 8])
    stat = k.sbuf("stat", [128, 8])
    ab = k.sbuf("ab", [128, 8 * 256], BF16)
    s1 = k.sbuf("s1", [128, 256], BF16)
    outf = k.sbuf("outf", [128, 1024])

    xl = k.dma("gpsimd", xnb[:, :], fm_ap(xn_i, 8))
    k.dma("gpsimd", gw_sb[:, :], bass.AP(gwT, 0, [[16, 128], [128 * 16, 8], [1, 16]]))
    ws1l = load_w_lhsT(k, ws1, wA, 8, DM)
    ws3l = load_w_lhsT(k, ws3, wB, 8, DM)
    ws2l = load_w_lhsT(k, ws2, wC, 8, DM)
    gwl = k.c("sg")

    for cb in range(4):
        def t_g(e, cb=cb):
            r = None
            for kc in range(8):
                r = e.matmul(cm.pp[0][0:16, :], gw_sb[:, kc * 16:(kc + 1) * 16],
                             xnb[:, kc * T + cb * 512: kc * T + cb * 512 + 512],
                             start=(kc == 0), stop=(kc == 7))
            return r
        k.t(t_g, wait=((("sg", gwl),) if cb == 0 else ()) + (("sv", k.c("sv")),))
        k.v(lambda e, cb=cb: e.tensor_copy(glog[0:16, cb * 512:(cb + 1) * 512], cm.pp[0][0:16, :]),
            wait=(("st", k.c("st")),))
    glog_v = k.c("sv")
    for t in range(NT):
        def t_tr(e, t=t):
            return e.transpose(cm.pp[1][:, 0:16], glog[0:16, t * 128:(t + 1) * 128],
                               cm.ident[0:16, 0:16])
        k.t(t_tr, wait=(("sv", max(glog_v, k.c("sv"))),))
        def v_p(e, t=t):
            e.tensor_copy(ptm[:, :], cm.pp[1][:, 0:16])
            e.tensor_reduce(stat[:, 0:1], ptm[:, :], axis=AX, op=ALU.max)
            return e.tensor_scalar(stat[:, 1:2], stat[:, 0:1], -1.0, None, ALU.mult)
        k.v(v_p, wait=(("st", k.c("st")),))
        k.s(lambda e: e.activation(ptm[:, :], ptm[:, :], AF.Exp, bias=stat[:, 1:2]),
            wait=(("sv", k.c("sv")),))
        def v_gate(e, t=t):
            e.tensor_reduce(stat[:, 2:3], ptm[:, :], axis=AX, op=ALU.add)
            e.reciprocal(stat[:, 2:3], stat[:, 2:3])
            e.tensor_scalar(ptm[:, :], ptm[:, :], stat[:, 2:3], None, ALU.mult)
            e.tensor_tensor(gtm[:, 0:8], bass.AP(ptm.ap().tensor, 0, [[16, 128], [2, 8]]),
                            bass.AP(ptm.ap().tensor, 1, [[16, 128], [2, 8]]), ALU.max)
            e.tensor_tensor(gtm[:, 0:4], bass.AP(gtm.ap().tensor, 0, [[8, 128], [2, 4]]),
                            bass.AP(gtm.ap().tensor, 1, [[8, 128], [2, 4]]), ALU.max)
            return e.tensor_reduce(stat[:, 3:4], gtm[:, 0:4], axis=AX, op=ALU.max)
        k.v(v_gate, wait=(("ss", k.c("ss")),))
        def v_gate2(e, t=t):
            e.tensor_tensor(gtm[:, 0:8], bass.AP(ptm.ap().tensor, 0, [[16, 128], [2, 8]]),
                            bass.AP(ptm.ap().tensor, 1, [[16, 128], [2, 8]]), ALU.max)
            e.tensor_tensor(gtm[:, 0:4], bass.AP(gtm.ap().tensor, 0, [[8, 128], [2, 4]]),
                            bass.AP(gtm.ap().tensor, 1, [[8, 128], [2, 4]]), ALU.max)
            e.tensor_scalar(gtm[:, 4:8], gtm[:, 0:4], stat[:, 3:4], None, ALU.is_ge)
            e.tensor_scalar(gtm[:, 4:8], gtm[:, 4:8], -1e30, None, ALU.mult)
            e.tensor_tensor(gtm[:, 4:8], gtm[:, 0:4], gtm[:, 4:8], ALU.add)
            e.tensor_reduce(stat[:, 4:5], gtm[:, 4:8], axis=AX, op=ALU.max)
            e.tensor_scalar(gtm[:, 4:8], gtm[:, 0:4], stat[:, 4:5], None, ALU.is_ge)
            for jj in range(4):
                e.tensor_copy(bass.AP(outf.ap().tensor, jj, [[1024, 128], [4, 4]]), gtm[:, 4:8])
            e.tensor_tensor(outf[:, 16:32], ptm[:, :], outf[:, 0:16], ALU.mult)
            e.tensor_copy(outf[:, 32:48], outf[:, 16:32])
            for _ in range(4):
                e.tensor_reduce(stat[:, 5:6], outf[:, 32:48], axis=AX, op=ALU.max)
                e.tensor_scalar(outf[:, 48:64], outf[:, 32:48], stat[:, 5:6], None, ALU.is_ge)
                e.tensor_scalar(outf[:, 48:64], outf[:, 48:64], -1e30, None, ALU.mult)
                e.tensor_tensor(outf[:, 32:48], outf[:, 32:48], outf[:, 48:64], ALU.add)
            e.tensor_scalar(outf[:, 48:64], outf[:, 16:32], stat[:, 5:6], None, ALU.is_ge)
            e.tensor_scalar(outf[:, 0:16], outf[:, 16:32], 0.0, None, ALU.is_gt)
            e.tensor_tensor(outf[:, 48:64], outf[:, 48:64], outf[:, 0:16], ALU.mult)
            return e.tensor_tensor(ptm[:, :], ptm[:, :], outf[:, 48:64], ALU.mult)
        k.v(v_gate2)
        def t_tr2(e, t=t):
            return e.transpose(cm.pp[1][0:16, 0:128], ptm[:, :], cm.ident[:, :])
        k.t(t_tr2, wait=(("sv", k.c("sv")),))
        k.v(lambda e, t=t: e.tensor_copy(ce_rows[0:16, t * 128:(t + 1) * 128],
                                         cm.pp[1][0:16, 0:128]),
            wait=(("st", k.c("st")),))
    ce_v = k.c("sv")

    def expert_pass(tag, nic, wga, wgb, wgo, init, prev_acc_map, e_i=None, wl=0):
        acc = {}
        for t in range(8):
            for ic in range(nic):
                def t_h(e, t=t, ic=ic, wl=wl):
                    if t == 0 and ic == 0 and wl:
                        e.wait_ge(k._sem["sg"], wl)
                    for kc in range(8):
                        e.matmul(cm.pp[0][:, 0:256],
                                 wga[:, kc * (nic * 128) + ic * 128: kc * (nic * 128) + ic * 128 + 128],
                                 xnb[:, kc * T + t * 256: kc * T + t * 256 + 256],
                                 start=(kc == 0), stop=(kc == 7))
                    r = None
                    for kc in range(8):
                        r = e.matmul(cm.pp[1][:, 0:256],
                                     wgb[:, kc * (nic * 128) + ic * 128: kc * (nic * 128) + ic * 128 + 128],
                                     xnb[:, kc * T + t * 256: kc * T + t * 256 + 256],
                                     start=(kc == 0), stop=(kc == 7))
                    return r
                k.t(t_h, wait=(("sv", k.c("sv")), ("ss", k.c("ss"))))
                k.s(lambda e: e.activation(s1[:, :], cm.pp[0][:, 0:256], AF.Silu),
                    wait=(("st", k.c("st")),))
                def v_a(e, ic=ic, t=t):
                    e.tensor_copy(ab[:, ic * 256:(ic + 1) * 256], cm.pp[1][:, 0:256])
                    r = e.tensor_tensor(ab[:, ic * 256:(ic + 1) * 256],
                                        ab[:, ic * 256:(ic + 1) * 256], s1[:, :], ALU.mult)
                    if e_i is not None:
                        r = e.tensor_tensor(ab[:, ic * 256:(ic + 1) * 256],
                                            ab[:, ic * 256:(ic + 1) * 256],
                                            ceb[:, t * 256:(t + 1) * 256], ALU.mult)
                    return r
                k.v(v_a, wait=(("ss", k.c("ss")),))
            h_v = k.c("sv")
            def t_o(e, nic=nic, h_v=h_v):
                r = None
                for oc in range(8):
                    ps = cm.pp[2 + oc // 2][:, (oc % 2) * 256:(oc % 2) * 256 + 256]
                    for ic in range(nic):
                        r = e.matmul(ps, wgo[:, ic * DM + oc * 128: ic * DM + oc * 128 + 128],
                                     ab[:, ic * 256:(ic + 1) * 256],
                                     start=(ic == 0), stop=(ic == nic - 1))
                return r
            k.t(t_o, wait=(("sv", h_v),))
            prev = prev_acc_map.get(t, 0)
            for half in range(2):
                def v_ev(e, half=half):
                    r = None
                    for oc in range(half * 4, half * 4 + 4):
                        ps = cm.pp[2 + oc // 2][:, (oc % 2) * 256:(oc % 2) * 256 + 256]
                        r = e.tensor_copy(outf[:, (oc % 4) * 256:(oc % 4) * 256 + 256], ps)
                    return r
                k.v(v_ev, wait=(("st", k.c("st")), ("sg", k.c("sg"))))
                kw = {} if init else {"accum_op": ALU.add}
                k.dma("gpsimd", bass.AP(f1_o, half * 4 * 128 * T + t * 256, [[T, 128], [128 * T, 4], [1, 256]]),
                      outf[:, :], wait=(("sv", k.c("sv")),) + ((("sg", prev),) if prev else ()),
                      **kw)
            acc[t] = k.c("sg")
        return acc

    def ep_shared():
        return expert_pass("sh", 8, wA, wB, wC, init_out,
                           {t: out_gate for t in range(8)}, None,
                           wl=max(ws1l, ws3l, ws2l, xl))
    acc = ep_shared()
    for e_i in range(16):
        w1l = k.dma("gpsimd", wA[:, 0:4096],
                    bass.AP(w1, e_i * DM * 512, [[512, 128], [128 * 512, 8], [1, 512]]),
                    wait=(("st", k.c("st")),))
        w3l = k.dma("gpsimd", wA[:, 4096:8192],
                    bass.AP(w3, e_i * DM * 512, [[512, 128], [128 * 512, 8], [1, 512]]),
                    wait=(("st", k.c("st")),))
        w2l = k.dma("gpsimd", wB[:, 0:4096],
                    bass.AP(w2, e_i * 512 * DM, [[DM, 128], [128 * DM, 4], [1, DM]]),
                    wait=(("st", k.c("st")),))
        cel = k.dma("sync", glog[0:1, :], ce_rows[e_i:e_i + 1, :],
                    wait=(("sv", max(ce_v, k.c("sv"))), ("st", k.c("st"))))
        for cb in range(4):
            def t_bc(e, cb=cb, cel=cel):
                e.wait_ge(k._sem["sd"], cel)
                return e.matmul(cm.pp[1][:, :], cm.ones_row[0:1, :],
                                glog[0:1, cb * 512:(cb + 1) * 512], start=True, stop=True)
            k.t(t_bc, wait=(("sv", k.c("sv")),))
            k.v(lambda e, cb=cb: e.tensor_copy(ceb[:, cb * 512:(cb + 1) * 512], cm.pp[1][:, :]),
                wait=(("st", k.c("st")),))
        def expert_one(e_i, acc):
            out = {}
            for t in range(8):
                for ic in range(4):
                    def t_h(e, t=t, ic=ic, w1l=w1l, w3l=w3l):
                        if t == 0 and ic == 0:
                            e.wait_ge(k._sem["sg"], max(w1l, w3l))
                        for kc in range(8):
                            e.matmul(cm.pp[0][:, 0:256],
                                     wA[:, kc * 512 + ic * 128: kc * 512 + ic * 128 + 128],
                                     xnb[:, kc * T + t * 256: kc * T + t * 256 + 256],
                                     start=(kc == 0), stop=(kc == 7))
                        r = None
                        for kc in range(8):
                            r = e.matmul(cm.pp[1][:, 0:256],
                                         wA[:, 4096 + kc * 512 + ic * 128: 4096 + kc * 512 + ic * 128 + 128],
                                         xnb[:, kc * T + t * 256: kc * T + t * 256 + 256],
                                         start=(kc == 0), stop=(kc == 7))
                        return r
                    k.t(t_h, wait=(("sv", k.c("sv")), ("ss", k.c("ss"))))
                    k.s(lambda e: e.activation(s1[:, :], cm.pp[0][:, 0:256], AF.Silu),
                        wait=(("st", k.c("st")),))
                    def v_a(e, ic=ic, t=t):
                        e.tensor_copy(ab[:, ic * 256:(ic + 1) * 256], cm.pp[1][:, 0:256])
                        e.tensor_tensor(ab[:, ic * 256:(ic + 1) * 256],
                                        ab[:, ic * 256:(ic + 1) * 256], s1[:, :], ALU.mult)
                        return e.tensor_tensor(ab[:, ic * 256:(ic + 1) * 256],
                                               ab[:, ic * 256:(ic + 1) * 256],
                                               ceb[:, t * 256:(t + 1) * 256], ALU.mult)
                    k.v(v_a, wait=(("ss", k.c("ss")),))
                h_v = k.c("sv")
                def t_o(e, h_v=h_v, w2l=w2l):
                    e.wait_ge(k._sem["sg"], w2l)
                    r = None
                    for oc in range(8):
                        ps = cm.pp[2 + oc // 2][:, (oc % 2) * 256:(oc % 2) * 256 + 256]
                        for ic in range(4):
                            r = e.matmul(ps, wB[:, ic * DM + oc * 128: ic * DM + oc * 128 + 128],
                                         ab[:, ic * 256:(ic + 1) * 256],
                                         start=(ic == 0), stop=(ic == 3))
                    return r
                k.t(t_o, wait=(("sv", h_v),))
                prev = acc.get(t, 0)
                for half in range(2):
                    def v_ev(e, half=half):
                        r = None
                        for oc in range(half * 4, half * 4 + 4):
                            ps = cm.pp[2 + oc // 2][:, (oc % 2) * 256:(oc % 2) * 256 + 256]
                            r = e.tensor_copy(outf[:, (oc % 4) * 256:(oc % 4) * 256 + 256], ps)
                        return r
                    k.v(v_ev, wait=(("st", k.c("st")), ("sg", k.c("sg"))))
                    k.dma("gpsimd", bass.AP(f1_o, half * 4 * 128 * T + t * 256, [[T, 128], [128 * T, 4], [1, 256]]),
                          outf[:, :], wait=(("sv", k.c("sv")), ("sg", prev)), accum_op=ALU.add)
                out[t] = k.c("sg")
            return out
        acc = expert_one(e_i, acc)
    k.on("gpsimd", lambda e: None, wait=(("sg", k.c("sg")), ("sd", k.c("sd"))))
    return k if standalone else None


GRID_D = [0.5, 0.5, 0.5, 0.5, 1.0, 1.0, 2.0]
MID_V = [0.25, 0.75, 1.25, 1.75, 2.5, 3.5, 5.0]


def emit_fp4(k, cm, src, dst, W, qs1, qs2, qs3, rinv, scl, wait=()):
    nb = W // 16
    q1w = qs1.ap().tensor.shape[1]
    q2w = qs2.ap().tensor.shape[1]
    k.s(lambda e: e.activation(qs1[:, 0:W], src[:, 0:W], AF.Abs),
        wait=wait + (("sv", k.c("sv")),))
    def v_all(e):
        e.tensor_tensor(qs2[:, 0:W // 2], bass.AP(qs1.ap().tensor, 0, [[q1w, 128], [2, W // 2]]),
                        bass.AP(qs1.ap().tensor, 1, [[q1w, 128], [2, W // 2]]), ALU.max)
        e.tensor_tensor(qs1[:, 0:W // 4], bass.AP(qs2.ap().tensor, 0, [[q2w, 128], [2, W // 4]]),
                        bass.AP(qs2.ap().tensor, 1, [[q2w, 128], [2, W // 4]]), ALU.max)
        e.tensor_tensor(qs2[:, 0:W // 8], bass.AP(qs1.ap().tensor, 0, [[q1w, 128], [2, W // 8]]),
                        bass.AP(qs1.ap().tensor, 1, [[q1w, 128], [2, W // 8]]), ALU.max)
        e.tensor_tensor(scl[:, 0:nb], bass.AP(qs2.ap().tensor, 0, [[q2w, 128], [2, nb]]),
                        bass.AP(qs2.ap().tensor, 1, [[q2w, 128], [2, nb]]), ALU.max)
        e.tensor_scalar(scl[:, 0:nb], scl[:, 0:nb], 1.0 / 6.0, None, ALU.mult)
        e.tensor_scalar(scl[:, 0:nb], scl[:, 0:nb], 1e-12, None, ALU.max)
        e.reciprocal(rinv[:, 0:nb], scl[:, 0:nb])
        for j in range(16):
            e.tensor_copy(bass.AP(qs2.ap().tensor, j, [[q2w, 128], [16, nb]]), rinv[:, 0:nb])
        e.tensor_tensor(qs2[:, 0:W], src[:, 0:W], qs2[:, 0:W], ALU.mult)
        return e.memset(qs3[:, 0:W], 0.0)
    k.v(v_all, wait=(("ss", k.c("ss")),))
    def s_as(e):
        e.activation(qs1[:, 0:W], qs2[:, 0:W], AF.Abs)
        return e.activation(qs2[:, 0:W], qs2[:, 0:W], AF.Sign)
    k.s(s_as, wait=(("sv", k.c("sv")),))
    def v_q(e):
        for i in range(7):
            e.tensor_scalar(dst[:, 0:W], qs1[:, 0:W], MID_V[i], GRID_D[i], ALU.is_gt, ALU.mult)
            e.tensor_tensor(qs3[:, 0:W], qs3[:, 0:W], dst[:, 0:W], ALU.add)
        e.tensor_tensor(qs3[:, 0:W], qs3[:, 0:W], qs2[:, 0:W], ALU.mult)
        for j in range(16):
            e.tensor_copy(bass.AP(qs2.ap().tensor, j, [[q2w, 128], [16, nb]]), scl[:, 0:nb])
        e.tensor_tensor(qs3[:, 0:W], qs3[:, 0:W], qs2[:, 0:W], ALU.mult)
        return e.tensor_copy(dst[:, 0:W], qs3[:, 0:W])
    k.v(v_q, wait=(("ss", k.c("ss")),))
    return k.c("sv")


def build_l2(k=None, cm=None, io=None, r2_ready=0):
    standalone = k is None
    if standalone:
        k = KB("l2")
        io = dict(R1_i=k.din("R1_i", [DM, T], F32), x4c_i=k.din("x4c_i", [DM, T], F32),
                  f1c_i=k.din("f1c_i", [DM, T], F32),
                  w_gate=k.din("w_gate", [2816, DM], F32), w_up=k.din("w_up", [2816, DM], F32),
                  w_down=k.din("w_down", [DM, 2816], F32),
                  R3_o=k.dout("R3_o", [DM, T], F32), R2_d=k.dtmp("R2_d", [DM, T], F32))
        cm = Common(k)
    w_gate = io["w_gate"]; w_up = io["w_up"]; w_down = io["w_down"]
    R3_o = io["R3_o"]; R2_d = io["R2_d"]
    wqg_d = k.dtmp(f"wqg_d_{KB._uid}", [2816, DM], BF16)
    wqu_d = k.dtmp(f"wqu_d_{KB._uid}", [2816, DM], BF16)
    wqd_d = k.dtmp(f"wqd_d_{KB._uid}", [DM, 2816], BF16)
    h_d = k.dtmp(f"h_d_{KB._uid}", [T, 2816], F32)
    sc_d = k.dtmp(f"sc_d_{KB._uid}", [16, PL], F32)
    xwin = k.sbuf("xwin", [128, 8 * 512])
    xnb = k.sbuf("xnb", [128, 8 * T], BF16)
    sq = k.sbuf("sq", [128, 512])
    rows = k.sbuf("rows", [1, 2 * T])
    mb = k.sbuf("mb", [128, T])
    rb = k.sbuf("rb", [128, T])
    qs1 = k.sbuf("qs1", [128, 2816])
    qs2 = k.sbuf("qs2", [128, 2816])
    qs3 = k.sbuf("qs3", [128, 2816])
    rinv = k.sbuf("rinv", [128, 176])
    scl = k.sbuf("scl", [128, 176])
    hbuf = k.sbuf("hbuf", [128, 2816])
    hq = k.sbuf("hq", [128, 2816], BF16)
    wrow = hbuf
    wqrow = hq
    xq_fm = k.sbuf("xq_fm", [128, 8 * T], BF16)
    xtm = k.sbuf("xtm", [128, DM], BF16)
    xqtm = k.sbuf("xqtm", [128, DM], BF16)
    strip_g = k.sbuf("strip_g", [128, 8 * 512], BF16)
    strip_u = k.sbuf("strip_u", [128, 8 * 512], BF16)
    s1 = k.sbuf("s1", [128, 512])
    hq_fm = k.sbuf("hq_fm", [128, 22 * 128], BF16)
    wdT = k.sbuf("wdT", [128, 6 * DM], BF16)
    sc_sb = qs1
    wrows = qs3
    wtm = k.sbuf("wtm", [128, 16])
    gtm2 = k.sbuf("gtm2", [128, 16])
    wrowb = k.sbuf("wrowb", [128, PL], BF16)
    accb = hbuf
    outq = qs2
    hqT_d = k.dtmp("hqT_d", [2816, T], BF16)

    if standalone:
        r2 = k.dma("gpsimd", fm_ap(R2_d, 8), fm_ap(io["R1_i"], 8))
        r2 = k.dma("gpsimd", fm_ap(R2_d, 8), fm_ap(io["x4c_i"], 8), wait=(("sg", r2),), accum_op=ALU.add)
        r2 = k.dma("gpsimd", fm_ap(R2_d, 8), fm_ap(io["f1c_i"], 8), wait=(("sg", r2),), accum_op=ALU.add)
    else:
        r2 = r2_ready
    r3i = k.dma("gpsimd", fm_ap(R3_o, 8), fm_ap(R2_d, 8), wait=(("sg", r2),))

    emit_ln_stream(k, cm, R2_d, 8, xwin, xnb, wait=(("sg", r2),),
                   sq_buf=sq, rows=rows, mb=mb, rb=rb)
    xn_v = k.c("sv")

    for bq in range(4):
        for bk in range(bq, 4):
            for ch in range(8):
                def v_pr(e, bq=bq, bk=bk, ch=ch):
                    return e.tensor_tensor(sq[:, 0:512],
                                           xnb[:, ch * T + bq * 512: ch * T + bq * 512 + 512],
                                           xnb[:, ch * T + bk * 512: ch * T + bk * 512 + 512],
                                           ALU.mult)
                w_extra = (("sg", k.c("sg")),) if ch == 0 else ()
                k.v(v_pr, wait=(("sv", xn_v), ("st", k.c("st"))) + w_extra)
                def t_one(e, ch=ch):
                    return e.matmul(cm.prow[0:1, 0:512], cm.ones_col[:, 0:1],
                                    sq[:, 0:512], start=(ch == 0), stop=(ch == 7))
                k.t(t_one, wait=(("sv", k.c("sv")),))
            k.v(lambda e: e.tensor_scalar(sq[0:1, 0:512], cm.prow[0:1, 0:512], 1.0 / 32.0, None, ALU.mult),
                wait=(("st", k.c("st")),))
            svv = k.c("sv")
            k.dma("gpsimd", bass.AP(sc_d, (bq * 4 + bk) * PL, [[PL, 1], [1, PL]]),
                  sq[0:1, 0:512], wait=(("sv", svv),))
            if bq != bk:
                k.dma("gpsimd", bass.AP(sc_d, (bk * 4 + bq) * PL, [[PL, 1], [1, PL]]),
                      sq[0:1, 0:512], wait=(("sv", svv),))
    scw = k.c("sg")
    scl_d = k.dma("sync", sc_sb[0:16, 0:PL], sc_d[:, :], wait=(("sg", scw),))
    for pt in range(4):
        def t_tr(e, pt=pt):
            e.wait_ge(k._sem["sd"], scl_d)
            return e.transpose(cm.pp[1][:, 0:16], sc_sb[0:16, pt * 128:(pt + 1) * 128],
                               cm.ident[0:16, 0:16])
        k.t(t_tr, wait=(("sv", k.c("sv")),))
        def v_sm(e, pt=pt):
            e.tensor_copy(wtm[:, :], cm.pp[1][:, 0:16])
            e.tensor_tensor(gtm2[:, 0:8], bass.AP(wtm.ap().tensor, 0, [[16, 128], [2, 8]]),
                            bass.AP(wtm.ap().tensor, 1, [[16, 128], [2, 8]]), ALU.max)
            e.tensor_tensor(gtm2[:, 8:12], bass.AP(gtm2.ap().tensor, 0, [[16, 128], [2, 4]]),
                            bass.AP(gtm2.ap().tensor, 1, [[16, 128], [2, 4]]), ALU.max)
            for jj in range(4):
                e.tensor_copy(bass.AP(gtm2.ap().tensor, jj, [[16, 128], [4, 4]]),
                              gtm2[:, 8:12])
            e.tensor_tensor(wtm[:, :], wtm[:, :], gtm2[:, 0:16], ALU.subtract)
            return wtm
        k.v(v_sm, wait=(("st", k.c("st")),))
        k.s(lambda e: e.activation(wtm[:, :], wtm[:, :], AF.Exp),
            wait=(("sv", k.c("sv")),))
        def v_sm2(e, pt=pt):
            e.tensor_tensor(gtm2[:, 0:8], bass.AP(wtm.ap().tensor, 0, [[16, 128], [2, 8]]),
                            bass.AP(wtm.ap().tensor, 1, [[16, 128], [2, 8]]), ALU.add)
            e.tensor_tensor(gtm2[:, 8:12], bass.AP(gtm2.ap().tensor, 0, [[16, 128], [2, 4]]),
                            bass.AP(gtm2.ap().tensor, 1, [[16, 128], [2, 4]]), ALU.add)
            e.reciprocal(gtm2[:, 8:12], gtm2[:, 8:12])
            for jj in range(4):
                e.tensor_copy(bass.AP(gtm2.ap().tensor, jj, [[16, 128], [4, 4]]),
                              gtm2[:, 8:12])
            return e.tensor_tensor(wtm[:, :], wtm[:, :], gtm2[:, 0:16], ALU.mult)
        k.v(v_sm2, wait=(("ss", k.c("ss")),))
        def t_tr2(e, pt=pt):
            return e.transpose(cm.pp[1][0:16, 0:128], wtm[:, :], cm.ident[:, :])
        k.t(t_tr2, wait=(("sv", k.c("sv")),))
        k.v(lambda e, pt=pt: e.tensor_copy(wrows[0:16, pt * 128:(pt + 1) * 128],
                                           cm.pp[1][0:16, 0:128]),
            wait=(("st", k.c("st")),))
    wr_v = k.c("sv")
    attn_acc = {}
    for bq in range(4):
        for chg in range(4):
            k.v(lambda e: e.memset(accb[:, 0:1024], 0.0), wait=(("sg", k.c("sg")),))
            for bk in range(4):
                cel = k.dma("sync", sq[0:1, 0:512], wrows[bq * 4 + bk:bq * 4 + bk + 1, 0:PL],
                            wait=(("sv", max(wr_v, k.c("sv"))),))
                def t_bc(e, cel=cel):
                    e.wait_ge(k._sem["sd"], cel)
                    return e.matmul(cm.pp[2][:, :], cm.ones_row[0:1, :], sq[0:1, 0:512],
                                    start=True, stop=True)
                k.t(t_bc, wait=(("sv", k.c("sv")),))
                k.v(lambda e: e.tensor_copy(wrowb[:, 0:512], cm.pp[2][:, :]),
                    wait=(("st", k.c("st")),))
                def v_acc(e, bk=bk, chg=chg):
                    r = None
                    for ci in range(2):
                        ch = chg * 2 + ci
                        e.tensor_tensor(sq[:, 0:512], xnb[:, ch * T + bk * 512: ch * T + bk * 512 + 512],
                                        wrowb[:, 0:512], ALU.mult)
                        r = e.tensor_tensor(accb[:, ci * 512:(ci + 1) * 512],
                                            accb[:, ci * 512:(ci + 1) * 512], sq[:, 0:512], ALU.add)
                    return r
                k.v(v_acc)
            k.dma("gpsimd", bass.AP(R3_o, chg * 2 * 128 * T + bq * 512, [[T, 128], [128 * T, 2], [1, 512]]),
                  accb[:, 0:1024], wait=(("sv", k.c("sv")), ("sg", r3i)), accum_op=ALU.add)
        attn_acc[bq] = k.c("sg")
    attn_all = k.c("sg")

    if standalone:
        for (w_in, w_out, nch, Wd) in ((w_gate, wqg_d, 22, DM), (w_up, wqu_d, 22, DM),
                                       (w_down, wqd_d, 8, 2816)):
            for ch in range(nch):
                rl = k.dma("sync", wrow[:, 0:Wd], bass.AP(w_in, ch * 128 * Wd, [[Wd, 128], [1, Wd]]),
                           wait=(("sv", k.c("sv")), ("sg", k.c("sg")), ("ss", k.c("ss"))))
                emit_fp4(k, cm, wrow, wqrow, Wd, qs1, qs2, qs3, rinv, scl,
                         wait=(("sd", rl), ("sg", k.c("sg"))))
                k.dma("gpsimd", bass.AP(w_out, ch * 128 * Wd, [[Wd, 128], [1, Wd]]),
                      wqrow[:, 0:Wd], wait=(("sv", k.c("sv")),))
        wq_done = k.c("sg")
        wqg_src, wqu_src, wqd_src = wqg_d, wqu_d, wqd_d
        qwait = ()
    else:
        wqg_agi = k.dtmp(f"wqg_agi_{KB._uid}", [384, DM], BF16)
        wqu_agi = k.dtmp(f"wqu_agi_{KB._uid}", [384, DM], BF16)
        wqd_agi = k.dtmp(f"wqd_agi_{KB._uid}", [128, 2816], BF16)
        wqg_ago = k.dtmp(f"wqg_ago_{KB._uid}", [8 * 384, DM], BF16)
        wqu_ago = k.dtmp(f"wqu_ago_{KB._uid}", [8 * 384, DM], BF16)
        wqd_ago = k.dtmp(f"wqd_ago_{KB._uid}", [8 * 128, 2816], BF16)
        for (w_in, agi, nsl, Wd) in ((io["wg_slice"], wqg_agi, 3, DM),
                                     (io["wu_slice"], wqu_agi, 3, DM),
                                     (io["wd_slice"], wqd_agi, 1, 2816)):
            for ch in range(nsl):
                rl = k.dma("sync", wrow[:, 0:Wd], bass.AP(w_in, ch * 128 * Wd, [[Wd, 128], [1, Wd]]),
                           wait=(("sv", k.c("sv")), ("sg", k.c("sg")), ("ss", k.c("ss"))))
                emit_fp4(k, cm, wrow, wqrow, Wd, qs1, qs2, qs3, rinv, scl,
                         wait=(("sd", rl), ("sg", k.c("sg"))))
                k.dma("gpsimd", bass.AP(agi, ch * 128 * Wd, [[Wd, 128], [1, Wd]]),
                      wqrow[:, 0:Wd], wait=(("sv", k.c("sv")),))
        RGq = [list(range(NCORE))]
        for sem, it, ot in (("ccq1", wqg_agi, wqg_ago), ("ccq2", wqu_agi, wqu_ago),
                            ("ccq3", wqd_agi, wqd_ago)):
            k.sem(sem)
            k.on("gpsimd",
                 (lambda e, i=it, o=ot: e.collective_compute(
                     "AllGather", ALU.bypass, replica_groups=RGq,
                     ins=[i.ap().opt()], outs=[o.ap().opt()])),
                 wait=(("sg", k.c("sg")), ("sd", k.c("sd"))), inc=((sem, 1),))
        wq_done = k.c("sg")
        wqg_src, wqu_src, wqd_src = wqg_ago, wqu_ago, wqd_ago
        qwait = (("ccq1", 1), ("ccq2", 1), ("ccq3", 1))
    for t in range(NT):
        for ch in range(8):
            def t_tr(e, t=t, ch=ch):
                return e.transpose(cm.pbf[:, 0:128],
                                   xnb[:, ch * T + t * 128: ch * T + t * 128 + 128], cm.identb[:, :])
            k.t(t_tr, wait=(("sv", k.c("sv")),))
            k.v(lambda e, ch=ch: e.tensor_copy(xtm[:, ch * 128:(ch + 1) * 128], cm.pbf[:, 0:128]),
                wait=(("st", k.c("st")),))
        emit_fp4(k, cm, xtm, xqtm, DM, qs1, qs2, qs3, rinv, scl, wait=(("st", k.c("st")),))
        for ch in range(8):
            def t_tr2(e, t=t, ch=ch):
                return e.transpose(cm.pbf[:, 0:128], xqtm[:, ch * 128:(ch + 1) * 128], cm.identb[:, :])
            k.t(t_tr2, wait=(("sv", k.c("sv")),))
            k.v(lambda e, t=t, ch=ch: e.tensor_copy(
                xq_fm[:, ch * T + t * 128: ch * T + t * 128 + 128], cm.pbf[:, 0:128]),
                wait=(("st", k.c("st")),))
    xq_v = k.c("sv")

    NIB = 6
    for ib in range(NIB):
        iw = 512 if ib < 5 else 256
        lds = []
        for dcc in range(8):
            lds.append(k.dma("sync", strip_g[:, dcc * 512: dcc * 512 + iw],
                             bass.AP(wqg_src, ib * 512 * DM + dcc * 128, [[DM, iw], [1, 128]]),
                             transpose=True, wait=(("st", k.c("st")),) + (qwait if (ib == 0 and dcc == 0) else ())))
            lds.append(k.dma("sync", strip_u[:, dcc * 512: dcc * 512 + iw],
                             bass.AP(wqu_src, ib * 512 * DM + dcc * 128, [[DM, iw], [1, 128]]),
                             transpose=True))
        ldd = max(lds)
        for t in range(NT):
            def t_gu(e, ib=ib, t=t, iw=iw, ldd=ldd):
                if t == 0:
                    e.wait_ge(k._sem["sd"], ldd)
                for dcc in range(8):
                    e.matmul(cm.pp[0][:, 0:iw],
                             xq_fm[:, dcc * T + t * 128: dcc * T + t * 128 + 128],
                             strip_g[:, dcc * 512: dcc * 512 + iw],
                             start=(dcc == 0), stop=(dcc == 7))
                r = None
                for dcc in range(8):
                    r = e.matmul(cm.pp[1][:, 0:iw],
                                 xq_fm[:, dcc * T + t * 128: dcc * T + t * 128 + 128],
                                 strip_u[:, dcc * 512: dcc * 512 + iw],
                                 start=(dcc == 0), stop=(dcc == 7))
                return r
            k.t(t_gu, wait=(("sv", max(xq_v, k.c("sv"))), ("ss", k.c("ss"))))
            k.s(lambda e, iw=iw: e.activation(s1[:, 0:iw], cm.pp[0][:, 0:iw], AF.Silu),
                wait=(("st", k.c("st")),))
            def v_h(e, iw=iw):
                return e.tensor_tensor(hbuf[:, 0:iw], cm.pp[1][:, 0:iw], s1[:, 0:iw], ALU.mult)
            k.v(v_h, wait=(("ss", k.c("ss")), ("sg", k.c("sg"))))
            k.dma("gpsimd", bass.AP(h_d, t * 128 * 2816 + ib * 512, [[2816, 128], [1, iw]]),
                  hbuf[:, 0:iw], wait=(("sv", k.c("sv")),))
    h_done = k.c("sg")

    for t in range(NT):
        hl = k.dma("sync", hbuf[:, 0:2816],
                   bass.AP(h_d, t * 128 * 2816, [[2816, 128], [1, 2816]]),
                   wait=(("sg", h_done), ("sv", k.c("sv"))))
        emit_fp4(k, cm, hbuf, hq, 2816, qs1, qs2, qs3, rinv, scl,
                 wait=(("sd", hl), ("sg", k.c("sg"))))
        for icc in range(22):
            def t_tr(e, icc=icc):
                return e.transpose(cm.pbf[:, 0:128], hq[:, icc * 128:(icc + 1) * 128], cm.identb[:, :])
            k.t(t_tr, wait=(("sv", k.c("sv")),))
            k.v(lambda e, icc=icc: e.tensor_copy(hq_fm[:, icc * 128:(icc + 1) * 128], cm.pbf[:, 0:128]),
                wait=(("st", k.c("st")), ("sg", k.c("sg"))))
        k.dma("gpsimd", bass.AP(hqT_d, t * 128, [[T, 128], [128 * T, 22], [1, 128]]),
              hq_fm[:, :], wait=(("sv", k.c("sv")),))
    hqT_done = k.c("sg")

    groups = [(0, 6), (6, 6), (12, 6), (18, 4)]
    for (icc0, ng) in groups:
        wdl = []
        for ii in range(ng):
            wdl.append(k.dma("sync", wdT[:, ii * DM:(ii + 1) * DM],
                             bass.AP(wqd_src, (icc0 + ii) * 128, [[2816, DM], [1, 128]]),
                             transpose=True,
                             wait=((("sg", wq_done), ("st", k.c("st"))) + qwait) if ii == 0 else ()))
        wdld = max(wdl)
        for t in range(NT):
            hfl = k.dma("sync", hq_fm[:, 0:ng * 128],
                        bass.AP(hqT_d, icc0 * 128 * T + t * 128, [[T, 128], [128 * T, ng], [1, 128]]),
                        wait=(("sg", hqT_done), ("st", k.c("st"))))
            def t_dn(e, t=t, wdld=wdld, hfl=hfl, ng=ng):
                e.wait_ge(k._sem["sd"], max(wdld, hfl))
                r = None
                for oc in range(8):
                    ps = cm.pp[2 + oc // 2][:, (oc % 2) * 128:(oc % 2) * 128 + 128]
                    for ii in range(ng):
                        r = e.matmul(ps, wdT[:, ii * DM + oc * 128: ii * DM + oc * 128 + 128],
                                     hq_fm[:, ii * 128:(ii + 1) * 128],
                                     start=(ii == 0), stop=(ii == ng - 1))
                return r
            k.t(t_dn, wait=(("sv", k.c("sv")),))
            def v_ev(e):
                r = None
                for oc in range(8):
                    ps = cm.pp[2 + oc // 2][:, (oc % 2) * 128:(oc % 2) * 128 + 128]
                    r = e.tensor_copy(outq[:, oc * 128:(oc + 1) * 128], ps)
                return r
            k.v(v_ev, wait=(("st", k.c("st")), ("sg", k.c("sg"))))
            prev = max(attn_acc[t // 4], k.c("sg"))
            k.dma("gpsimd", bass.AP(R3_o, t * 128, [[T, 128], [128 * T, 8], [1, 128]]),
                  outq[:, 0:1024], wait=(("sv", k.c("sv")), ("sg", prev)), accum_op=ALU.add)
    k.on("gpsimd", lambda e: None, wait=(("sg", k.c("sg")), ("sd", k.c("sd"))))
    return k if standalone else None


    return k


def build_l3_pre(k=None, cm=None, io=None):
    standalone = k is None
    if standalone:
        k = KB("l3pre")
        io = dict(R3_i=k.din("R3_i", [DM, T], F32), wT=k.din("wT", [DM, 2 * DM], F32),
                  xn_o=k.dout("xn_o", [DM, T], F32), K_o=k.dout("K_o", [DM, T], BF16),
                  V_o=k.dout("V_o", [T, DM], BF16))
        cm = Common(k)
    R3_i = io["R3_i"]; wT = io["wT"]; xn_o = io["xn_o"]; K_o = io["K_o"]; V_o = io["V_o"]
    xwin = k.sbuf("xwin", [128, 8 * 512])
    xnb = k.sbuf("xnb", [128, 8 * T], BF16)
    sq = k.sbuf("sq", [128, 512])
    rows = k.sbuf("rows", [1, 2 * T])
    mb = k.sbuf("mb", [128, T])
    rb = k.sbuf("rb", [128, T])
    wk_sb = k.sbuf("wk_sb", [128, 8 * DM], BF16)
    wv_sb = k.sbuf("wv_sb", [128, 8 * DM], BF16)
    ev = k.sbuf("ev", [128, 512], BF16)

    kl = k.dma("gpsimd", wk_sb[:, :],
               bass.AP(wT, 0, [[2 * DM, 128], [128 * 2 * DM, 8], [1, DM]]))
    vl = k.dma("gpsimd", wv_sb[:, :],
               bass.AP(wT, DM, [[2 * DM, 128], [128 * 2 * DM, 8], [1, DM]]))
    emit_ln_stream(k, cm, R3_i, 8, xwin, xnb, xn_dram=xn_o,
                   sq_buf=sq, rows=rows, mb=mb, rb=rb)
    xn_v = k.c("sv")
    for mc in range(8):
        for cb in range(4):
            def t_k(e, mc=mc, cb=cb):
                r = None
                for kc in range(8):
                    r = e.matmul(cm.pp[0][:, :],
                                 wk_sb[:, kc * DM + mc * 128: kc * DM + mc * 128 + 128],
                                 xnb[:, kc * T + cb * 512: kc * T + cb * 512 + 512],
                                 start=(kc == 0), stop=(kc == 7))
                return r
            w0 = (("sv", xn_v), ("sg", kl)) if (mc == 0 and cb == 0) else ()
            k.t(t_k, wait=w0 + (("sg", k.c("sg")), ("sv", k.c("sv"))))
            k.v(lambda e: e.tensor_copy(ev[:, :], cm.pp[0][:, :]), wait=(("st", k.c("st")),))
            k.dma("gpsimd", bass.AP(K_o, mc * 128 * T + cb * 512, [[T, 128], [1, 512]]),
                  ev[:, :], wait=(("sv", k.c("sv")),))
    for t in range(NT):
        for nb in range(2):
            def t_vv(e, t=t, nb=nb):
                r = None
                for kc in range(8):
                    r = e.matmul(cm.pp[1][:, :],
                                 xnb[:, kc * T + t * 128: kc * T + t * 128 + 128],
                                 wv_sb[:, kc * DM + nb * 512: kc * DM + nb * 512 + 512],
                                 start=(kc == 0), stop=(kc == 7))
                return r
            w0 = (("sg", vl),) if (t == 0 and nb == 0) else ()
            k.t(t_vv, wait=w0 + (("sg", k.c("sg")), ("sv", k.c("sv"))))
            k.v(lambda e: e.tensor_copy(ev[:, :], cm.pp[1][:, :]), wait=(("st", k.c("st")),))
            k.dma("gpsimd", bass.AP(V_o, t * 128 * DM + nb * 512, [[DM, 128], [1, 512]]),
                  ev[:, :], wait=(("sv", k.c("sv")),))
    k.on("gpsimd", lambda e: None, wait=(("sg", k.c("sg")), ("sd", k.c("sd"))))
    return k if standalone else None


def build_l3_attn(k=None, cm=None, io=None, fused_kv=False):
    standalone = k is None
    if standalone:
        k = KB("l3attn")
        io = dict(q_i=k.din("q_i", [4, DM, 512], F32), K_i=k.din("K_i", [DM, 4 * L], BF16),
                  V_i=k.din("V_i", [4 * L, DM], BF16),
                  cmask=k.din("cmask", [4, 4, 8, 128, 512], BF16),
                  pv_o=k.dout("pv_o", [4, 512, DM], F32))
        cm = Common(k)
    cmask = io["cmask"]; pv_o = io["pv_o"]
    qb = k.sbuf("qb", [128, 8 * 512], BF16)
    Kb = k.sbuf("Kb", [128, 8 * 512], BF16)
    Vb = k.sbuf("Vb", [128, 4 * DM], BF16)
    msk = k.sbuf("msk", [128, 512], BF16)
    ssb = k.sbuf("ssb", [128, 512])
    P_sb = k.sbuf("P_sb", [128, 512], BF16)
    PTt = k.sbuf("PTt", [128, 4 * 128], BF16)
    mrow = k.sbuf("mrow", [128, 16])
    accq = [k.sbuf(f"accq{i}", [128, DM]) for i in range(4)]
    outb = k.sbuf("outb", [128, DM])

    def q_ap(u):
        if fused_kv:
            return bass.AP(io["xn3_d"], u * 512, [[T, 128], [128 * T, 8], [1, 512]])
        return bass.AP(io["q_i"], u * DM * 512, [[512, 128], [128 * 512, 8], [1, 512]])

    def k_ap(u, kb):
        if fused_kv:
            return bass.AP(io["kag"], kb * DM * T + u * 512, [[T, 128], [128 * T, 8], [1, 512]])
        return bass.AP(io["K_i"], u * L + kb * 512, [[4 * L, 128], [128 * 4 * L, 8], [1, 512]])

    def v_ap(u, kb):
        if fused_kv:
            return bass.AP(io["vag"], (kb * T + u * 512) * DM, [[DM, 128], [128 * DM, 4], [1, DM]])
        return bass.AP(io["V_i"], (u * L + kb * 512) * DM, [[DM, 128], [128 * DM, 4], [1, DM]])

    for u in range(4):
        ql = k.dma("gpsimd", qb[:, :], q_ap(u), wait=(("st", k.c("st")),))
        k.v(lambda e: e.memset(mrow[:, 0:4], -1e30), wait=(("ss", k.c("ss")),))
        for kb in range(8):
            kld = k.dma("sync", Kb[:, :], k_ap(u, kb),
                        wait=(("st", k.c("st")),))
            for qt in range(4):
                def t_s(e, qt=qt, kld=kld, ql=ql):
                    e.wait_ge(k._sem["sd"], kld)
                    e.wait_ge(k._sem["sg"], ql)
                    r = None
                    for kc in range(8):
                        r = e.matmul(cm.pp[0][:, :],
                                     qb[:, kc * 512 + qt * 128: kc * 512 + qt * 128 + 128],
                                     Kb[:, kc * 512:(kc + 1) * 512],
                                     start=(kc == 0), stop=(kc == 7))
                    return r
                k.t(t_s, wait=(("sv", k.c("sv")),))
                ml = k.dma("sync", msk[:, :],
                           bass.AP(cmask, ((u * 4 + qt) * 8 + kb) * 128 * 512, [[512, 128], [1, 512]]),
                           wait=(("sv", k.c("sv")),))
                def v_m(e, qt=qt, ml=ml):
                    e.wait_ge(k._sem["sd"], ml)
                    e.tensor_tensor(ssb[:, :], cm.pp[0][:, :], msk[:, :], ALU.add)
                    e.tensor_reduce(mrow[:, 8 + qt:9 + qt], ssb[:, :], axis=AX, op=ALU.max)
                    return e.tensor_tensor(mrow[:, qt:qt + 1], mrow[:, qt:qt + 1],
                                           mrow[:, 8 + qt:9 + qt], ALU.max)
                k.v(v_m, wait=(("st", k.c("st")),))
        def v_neg(e):
            e.tensor_scalar(mrow[:, 4:8], mrow[:, 0:4], -1.0, None, ALU.mult)
            e.memset(mrow[:, 8:12], 0.0)
            for i in range(4):
                e.memset(accq[i][:, :], 0.0)
            return e.memset(mrow[:, 12:16], 0.0)
        k.v(v_neg)
        for kb in range(8):
            kld = k.dma("sync", Kb[:, :], k_ap(u, kb),
                        wait=(("st", k.c("st")),))
            vld = k.dma("sync", Vb[:, :], v_ap(u, kb),
                        wait=(("st", k.c("st")),))
            for qt in range(4):
                def t_s2(e, qt=qt, kld=kld, vld=vld):
                    e.wait_ge(k._sem["sd"], max(kld, vld))
                    r = None
                    for kc in range(8):
                        r = e.matmul(cm.pp[0][:, :],
                                     qb[:, kc * 512 + qt * 128: kc * 512 + qt * 128 + 128],
                                     Kb[:, kc * 512:(kc + 1) * 512],
                                     start=(kc == 0), stop=(kc == 7))
                    return r
                k.t(t_s2, wait=(("sv", k.c("sv")), ("ss", k.c("ss"))))
                ml = k.dma("sync", msk[:, :],
                           bass.AP(cmask, ((u * 4 + qt) * 8 + kb) * 128 * 512, [[512, 128], [1, 512]]),
                           wait=(("sv", k.c("sv")),))
                def v_m2(e, ml=ml):
                    e.wait_ge(k._sem["sd"], ml)
                    return e.tensor_tensor(ssb[:, :], cm.pp[0][:, :], msk[:, :], ALU.add)
                k.v(v_m2, wait=(("st", k.c("st")),))
                def s_e(e, qt=qt):
                    return e.activation(P_sb[:, :], ssb[:, :], AF.Exp,
                                        bias=mrow[:, 4 + qt:5 + qt],
                                        accum_out=mrow[:, 12 + qt:13 + qt])
                k.s(s_e, wait=(("sv", k.c("sv")), ("st", k.c("st"))))
                def v_ss(e, qt=qt):
                    return e.tensor_tensor(mrow[:, 8 + qt:9 + qt], mrow[:, 8 + qt:9 + qt],
                                           mrow[:, 12 + qt:13 + qt], ALU.add)
                k.v(v_ss, wait=(("ss", k.c("ss")),))
                for kt in range(4):
                    def t_tr(e, kt=kt):
                        return e.transpose(cm.pbf[:, 0:128], P_sb[:, kt * 128:(kt + 1) * 128],
                                           cm.identb[:, :])
                    k.t(t_tr, wait=(("sv", k.c("sv")),))
                    k.v(lambda e, kt=kt: e.tensor_copy(PTt[:, kt * 128:(kt + 1) * 128],
                                                       cm.pbf[:, 0:128]),
                        wait=(("st", k.c("st")),))
                pt_v = k.c("sv")
                def t_pv(e, pt_v=pt_v):
                    r = None
                    for nb in range(2):
                        for kt in range(4):
                            r = e.matmul(cm.pp[1 + nb][:, :],
                                         PTt[:, kt * 128:(kt + 1) * 128],
                                         Vb[:, kt * DM + nb * 512: kt * DM + nb * 512 + 512],
                                         start=(kt == 0), stop=(kt == 3))
                    return r
                k.t(t_pv, wait=(("sv", pt_v),))
                def v_acc(e, qt=qt):
                    e.tensor_tensor(accq[qt][:, 0:512], accq[qt][:, 0:512], cm.pp[1][:, :], ALU.add)
                    return e.tensor_tensor(accq[qt][:, 512:1024], accq[qt][:, 512:1024],
                                           cm.pp[2][:, :], ALU.add)
                k.v(v_acc, wait=(("st", k.c("st")),))
        for qt in range(4):
            def v_out(e, qt=qt):
                e.reciprocal(mrow[:, 12 + qt:13 + qt], mrow[:, 8 + qt:9 + qt])
                return e.tensor_scalar(outb[:, :], accq[qt][:, :], mrow[:, 12 + qt:13 + qt],
                                       None, ALU.mult)
            k.v(v_out, wait=(("sg", k.c("sg")),))
            k.dma("gpsimd", bass.AP(pv_o, u * 512 * DM + qt * 128 * DM, [[DM, 128], [1, DM]]),
                  outb[:, :], wait=(("sv", k.c("sv")),))
    k.on("gpsimd", lambda e: None, wait=(("sg", k.c("sg")), ("sd", k.c("sd"))))
    return k if standalone else None


def build_l3_post(k=None, cm=None, io=None, fused_pv=False):
    standalone = k is None
    if standalone:
        k = KB("l3post")
        io = dict(R3_i=k.din("R3_i", [DM, T], F32), xn3_i=k.din("xn3_i", [DM, T], F32),
                  pvc_i=k.din("pvc_i", [DM, T], F32),
                  router_w=k.din("router_w", [DM, 8], F32),
                  w1=k.din("w1", [8, DM, DM], F32), w2=k.din("w2", [8, DM, DM], F32),
                  sgu=k.din("sgu", [DM, 2 * DM], F32), sdwn=k.din("sdwn", [DM, DM], F32),
                  sgw=k.din("sgw", [DM, 1], F32), y_o=k.dout("y_o", [T, DM], F32))
        cm = Common(k)
    R3_i = io["R3_i"]; xn3_i = io["xn3_i"]
    router_w = io["router_w"]; w1 = io["w1"]; w2 = io["w2"]
    sgu = io["sgu"]; sdwn = io["sdwn"]; sgw = io["sgw"]; y_o = io["y_o"]
    R4_d = k.dtmp(f"R4_d_{KB._uid}", [DM, T], F32)
    xnb = k.sbuf("xnb", [128, 8 * T], BF16)
    wA = k.sbuf("wA", [128, 8 * DM], BF16)
    wB = k.sbuf("wB", [128, 8 * DM], BF16)
    wC = k.sbuf("wC", [128, 8 * DM], BF16)
    gw_sb = k.sbuf("gw_sb", [128, 8 * 8], BF16)
    sgw_sb = k.sbuf("sgw_sb", [128, 8], BF16)
    glog = k.sbuf("glog", [8, T])
    srow = k.sbuf("srow", [1, T])
    ce_rows = k.sbuf("ce_rows", [8, T])
    ceb = k.sbuf("ceb", [128, T], BF16)
    sgb = k.sbuf("sgb", [128, 256], BF16)
    ptm = k.sbuf("ptm", [128, 8])
    stat = k.sbuf("stat", [128, 8])
    ab = k.sbuf("ab", [128, 8 * 256], BF16)
    s1 = k.sbuf("s1", [128, 256], BF16)
    outf = k.sbuf("outf", [128, 1024])
    ytm = k.sbuf("ytm", [128, DM])

    r4 = k.dma("gpsimd", fm_ap(R4_d, 8), fm_ap(R3_i, 8))
    r4 = k.dma("gpsimd", fm_ap(R4_d, 8), fm_ap(xn3_i, 8), wait=(("sg", r4),), accum_op=ALU.add)
    if not fused_pv:
        r4 = k.dma("gpsimd", fm_ap(R4_d, 8), fm_ap(io["pvc_i"], 8), wait=(("sg", r4),), accum_op=ALU.add)
    else:
        pv_d = io["pv_d"]
        ptt = k.sbuf("ptt", [128, DM])
        ptf = k.sbuf("ptf", [128, DM])
        for u in range(4):
            for qt in range(4):
                pl_ = k.dma("sync", ptt[:, :],
                            bass.AP(pv_d, (u * 512 + qt * 128) * DM, [[DM, 128], [1, DM]]),
                            wait=(("st", k.c("st")),))
                for ch in range(8):
                    def t_tr(e, ch=ch, pl_=pl_):
                        e.wait_ge(k._sem["sd"], pl_)
                        return e.transpose(cm.pp[5][:, 0:128], ptt[:, ch * 128:(ch + 1) * 128],
                                           cm.ident[:, :])
                    k.t(t_tr, wait=(("sv", k.c("sv")),))
                    k.v(lambda e, ch=ch: e.tensor_copy(ptf[:, ch * 128:(ch + 1) * 128],
                                                       cm.pp[5][:, 0:128]),
                        wait=(("st", k.c("st")), ("sg", k.c("sg"))))
                k.dma("gpsimd",
                      bass.AP(R4_d, u * 512 + qt * 128, [[T, 128], [128 * T, 8], [1, 128]]),
                      ptf[:, :], wait=(("sv", k.c("sv")), ("sg", r4)), accum_op=ALU.add)
        r4 = k.c("sg")
    xl = k.dma("gpsimd", xnb[:, :], fm_ap(xn3_i, 8))
    k.dma("gpsimd", gw_sb[:, :], bass.AP(router_w, 0, [[8, 128], [128 * 8, 8], [1, 8]]))
    k.dma("gpsimd", sgw_sb[:, 0:8], bass.AP(sgw, 0, [[1, 128], [128, 8]]),
          allow_slow_non_contiguous=True)
    gwl = k.c("sg"); sgl = k.c("sg")

    for cb in range(4):
        def t_g(e, cb=cb):
            r = None
            for kc in range(8):
                r = e.matmul(cm.pp[0][0:8, :], gw_sb[:, kc * 8:(kc + 1) * 8],
                             xnb[:, kc * T + cb * 512: cb * 512 + kc * T + 512],
                             start=(kc == 0), stop=(kc == 7))
            return r
        k.t(t_g, wait=((("sg", max(gwl, xl)),) if cb == 0 else ()) + (("sv", k.c("sv")),))
        k.v(lambda e, cb=cb: e.tensor_copy(glog[0:8, cb * 512:(cb + 1) * 512], cm.pp[0][0:8, :]),
            wait=(("st", k.c("st")),))
    glog_v = k.c("sv")
    for t in range(NT):
        def t_tr(e, t=t):
            return e.transpose(cm.pp[1][:, 0:8], glog[0:8, t * 128:(t + 1) * 128],
                               cm.ident[0:8, 0:8])
        k.t(t_tr, wait=(("sv", max(glog_v, k.c("sv"))),))
        def v_p(e, t=t):
            e.tensor_copy(ptm[:, :], cm.pp[1][:, 0:8])
            e.tensor_reduce(stat[:, 0:1], ptm[:, :], axis=AX, op=ALU.max)
            return e.tensor_scalar(stat[:, 1:2], stat[:, 0:1], -1.0, None, ALU.mult)
        k.v(v_p, wait=(("st", k.c("st")),))
        k.s(lambda e: e.activation(ptm[:, :], ptm[:, :], AF.Exp, bias=stat[:, 1:2]),
            wait=(("sv", k.c("sv")),))
        def v_sm(e, t=t):
            e.tensor_reduce(stat[:, 2:3], ptm[:, :], axis=AX, op=ALU.add)
            e.reciprocal(stat[:, 2:3], stat[:, 2:3])
            e.tensor_scalar(ptm[:, :], ptm[:, :], stat[:, 2:3], None, ALU.mult)
            e.tensor_reduce(stat[:, 3:4], ptm[:, :], axis=AX, op=ALU.max)
            e.tensor_scalar(outf[:, 0:8], ptm[:, :], stat[:, 3:4], None, ALU.is_ge)
            e.tensor_scalar(outf[:, 8:16], outf[:, 0:8], -1e30, None, ALU.mult)
            e.tensor_tensor(outf[:, 8:16], ptm[:, :], outf[:, 8:16], ALU.add)
            e.tensor_reduce(stat[:, 5:6], outf[:, 8:16], axis=AX, op=ALU.max)
            e.tensor_scalar(outf[:, 0:8], ptm[:, :], stat[:, 5:6], None, ALU.is_ge)
            return e.tensor_tensor(ptm[:, :], ptm[:, :], outf[:, 0:8], ALU.mult)
        k.v(v_sm, wait=(("ss", k.c("ss")),))
        def t_tr2(e, t=t):
            return e.transpose(cm.pp[1][0:8, 0:128], ptm[:, :], cm.ident[:, :])
        k.t(t_tr2, wait=(("sv", k.c("sv")),))
        k.v(lambda e, t=t: e.tensor_copy(ce_rows[0:8, t * 128:(t + 1) * 128],
                                         cm.pp[1][0:8, 0:128]),
            wait=(("st", k.c("st")),))
    ce_v = k.c("sv")

    for cb in range(4):
        def t_sg(e, cb=cb):
            r = None
            for kc in range(8):
                r = e.matmul(cm.prow[0:1, :], sgw_sb[:, kc:kc + 1],
                             xnb[:, kc * T + cb * 512: kc * T + cb * 512 + 512],
                             start=(kc == 0), stop=(kc == 7))
            return r
        k.t(t_sg, wait=(("sg", sgl), ("sv", k.c("sv")), ("ss", k.c("ss"))))
        k.s(lambda e, cb=cb: e.activation(srow[0:1, cb * 512:(cb + 1) * 512],
                                          cm.prow[0:1, :], AF.Sigmoid),
            wait=(("st", k.c("st")),))
    srow_s = k.c("ss")

    sl1 = k.dma("gpsimd", wA[:, :],
                bass.AP(sgu, 0, [[2 * DM, 128], [128 * 2 * DM, 8], [1, DM]]))
    sl2 = k.dma("gpsimd", wB[:, :],
                bass.AP(sgu, DM, [[2 * DM, 128], [128 * 2 * DM, 8], [1, DM]]))
    sl3 = load_w_lhsT(k, sdwn, wC, 8, DM)
    acc = {}
    for t in range(8):
        def t_bb(e, t=t):
            return e.matmul(cm.pp[1][:, 0:256], cm.ones_row[0:1, :],
                            srow[0:1, t * 256:(t + 1) * 256], start=True, stop=True)
        k.t(t_bb, wait=(("ss", srow_s), ("sv", k.c("sv"))))
        k.v(lambda e: e.tensor_copy(sgb[:, :], cm.pp[1][:, 0:256]),
            wait=(("st", k.c("st")),))
        for ic in range(8):
            def t_h(e, t=t, ic=ic):
                if t == 0 and ic == 0:
                    e.wait_ge(k._sem["sg"], max(sl1, sl2, sl3))
                for kc in range(8):
                    e.matmul(cm.pp[0][:, 0:256],
                             wA[:, kc * DM + ic * 128: kc * DM + ic * 128 + 128],
                             xnb[:, kc * T + t * 256: kc * T + t * 256 + 256],
                             start=(kc == 0), stop=(kc == 7))
                r = None
                for kc in range(8):
                    r = e.matmul(cm.pp[1][:, 0:256],
                                 wB[:, kc * DM + ic * 128: kc * DM + ic * 128 + 128],
                                 xnb[:, kc * T + t * 256: kc * T + t * 256 + 256],
                                 start=(kc == 0), stop=(kc == 7))
                return r
            k.t(t_h, wait=(("sv", k.c("sv")), ("ss", k.c("ss"))))
            k.s(lambda e: e.activation(s1[:, :], cm.pp[0][:, 0:256], AF.Silu),
                wait=(("st", k.c("st")),))
            def v_a(e, ic=ic):
                e.tensor_copy(ab[:, ic * 256:(ic + 1) * 256], cm.pp[1][:, 0:256])
                return e.tensor_tensor(ab[:, ic * 256:(ic + 1) * 256],
                                       ab[:, ic * 256:(ic + 1) * 256], s1[:, :], ALU.mult)
            k.v(v_a, wait=(("ss", k.c("ss")),))
        h_v = k.c("sv")
        def t_o(e):
            r = None
            for oc in range(8):
                ps = cm.pp[2 + oc // 2][:, (oc % 2) * 256:(oc % 2) * 256 + 256]
                for ic in range(8):
                    r = e.matmul(ps, wC[:, ic * DM + oc * 128: ic * DM + oc * 128 + 128],
                                 ab[:, ic * 256:(ic + 1) * 256],
                                 start=(ic == 0), stop=(ic == 7))
            return r
        k.t(t_o, wait=(("sv", h_v),))
        for half in range(2):
            def v_ev(e, half=half):
                r = None
                for oc in range(half * 4, half * 4 + 4):
                    ps = cm.pp[2 + oc // 2][:, (oc % 2) * 256:(oc % 2) * 256 + 256]
                    e.tensor_copy(outf[:, (oc % 4) * 256:(oc % 4) * 256 + 256], ps)
                    r = e.tensor_tensor(outf[:, (oc % 4) * 256:(oc % 4) * 256 + 256],
                                        outf[:, (oc % 4) * 256:(oc % 4) * 256 + 256],
                                        sgb[:, :], ALU.mult)
                return r
            k.v(v_ev, wait=(("st", k.c("st")), ("sg", k.c("sg"))))
            k.dma("gpsimd", bass.AP(R4_d, half * 4 * 128 * T + t * 256, [[T, 128], [128 * T, 4], [1, 256]]),
                  outf[:, :], wait=(("sv", k.c("sv")), ("sg", r4)), accum_op=ALU.add)
        acc[t] = k.c("sg")

    for e_i in range(8):
        w1l = k.dma("gpsimd", wA[:, :],
                    bass.AP(w1, e_i * DM * DM, [[DM, 128], [128 * DM, 8], [1, DM]]),
                    wait=(("st", k.c("st")),))
        w2l = k.dma("gpsimd", wB[:, :],
                    bass.AP(w2, e_i * DM * DM, [[DM, 128], [128 * DM, 8], [1, DM]]),
                    wait=(("st", k.c("st")),))
        ce_stage = glog
        cel = k.dma("sync", ce_stage[0:1, :], ce_rows[e_i:e_i + 1, :],
                    wait=(("sv", max(ce_v, k.c("sv"))), ("st", k.c("st"))))
        for cb in range(4):
            def t_bc(e, cb=cb, cel=cel):
                e.wait_ge(k._sem["sd"], cel)
                return e.matmul(cm.pp[1][:, :], cm.ones_row[0:1, :],
                                ce_stage[0:1, cb * 512:(cb + 1) * 512], start=True, stop=True)
            k.t(t_bc, wait=(("sv", k.c("sv")),))
            k.v(lambda e, cb=cb: e.tensor_copy(ceb[:, cb * 512:(cb + 1) * 512], cm.pp[1][:, :]),
                wait=(("st", k.c("st")),))
        for t in range(8):
            for ic in range(8):
                def t_h(e, t=t, ic=ic, w1l=w1l):
                    if t == 0 and ic == 0:
                        e.wait_ge(k._sem["sg"], w1l)
                    r = None
                    for kc in range(8):
                        r = e.matmul(cm.pp[0][:, 0:256],
                                     wA[:, kc * DM + ic * 128: kc * DM + ic * 128 + 128],
                                     xnb[:, kc * T + t * 256: kc * T + t * 256 + 256],
                                     start=(kc == 0), stop=(kc == 7))
                    return r
                k.t(t_h, wait=(("sv", k.c("sv")), ("ss", k.c("ss"))))
                k.s(lambda e, ic=ic: e.activation(ab[:, ic * 256:(ic + 1) * 256],
                                                  cm.pp[0][:, 0:256], AF.Silu),
                    wait=(("st", k.c("st")),))
                k.v(lambda e, ic=ic, t=t: e.tensor_tensor(
                    ab[:, ic * 256:(ic + 1) * 256], ab[:, ic * 256:(ic + 1) * 256],
                    ceb[:, t * 256:(t + 1) * 256], ALU.mult),
                    wait=(("ss", k.c("ss")),))
            h_v = k.c("sv")
            def t_o(e, w2l=w2l, h_v=h_v):
                e.wait_ge(k._sem["sg"], w2l)
                r = None
                for oc in range(8):
                    ps = cm.pp[2 + oc // 2][:, (oc % 2) * 256:(oc % 2) * 256 + 256]
                    for ic in range(8):
                        r = e.matmul(ps, wB[:, ic * DM + oc * 128: ic * DM + oc * 128 + 128],
                                     ab[:, ic * 256:(ic + 1) * 256],
                                     start=(ic == 0), stop=(ic == 7))
                return r
            k.t(t_o, wait=(("sv", h_v),))
            prev = acc[t]
            for half in range(2):
                def v_ev(e, half=half):
                    r = None
                    for oc in range(half * 4, half * 4 + 4):
                        ps = cm.pp[2 + oc // 2][:, (oc % 2) * 256:(oc % 2) * 256 + 256]
                        r = e.tensor_copy(outf[:, (oc % 4) * 256:(oc % 4) * 256 + 256], ps)
                    return r
                k.v(v_ev, wait=(("st", k.c("st")), ("sg", k.c("sg"))))
                k.dma("gpsimd", bass.AP(R4_d, half * 4 * 128 * T + t * 256, [[T, 128], [128 * T, 4], [1, 256]]),
                      outf[:, :], wait=(("sv", k.c("sv")), ("sg", prev)), accum_op=ALU.add)
            acc[t] = k.c("sg")
    all_acc = k.c("sg")

    for t in range(NT):
        tl = k.dma("sync", outf[:, 0:1024],
                   bass.AP(R4_d, t * 128, [[T, 128], [128 * T, 8], [1, 128]]),
                   wait=(("sg", all_acc), ("st", k.c("st"))))
        for ch in range(8):
            def t_tr(e, ch=ch, tl=tl):
                e.wait_ge(k._sem["sd"], tl)
                return e.transpose(cm.pp[5][:, 0:128], outf[:, ch * 128:(ch + 1) * 128],
                                   cm.ident[:, :])
            k.t(t_tr, wait=(("sv", k.c("sv")),))
            k.v(lambda e, ch=ch: e.tensor_copy(ytm[:, ch * 128:(ch + 1) * 128], cm.pp[5][:, 0:128]),
                wait=(("st", k.c("st")),))
        k.dma("gpsimd", y_o[t * 128:(t + 1) * 128, :], ytm[:, :], wait=(("sv", k.c("sv")),))
    k.on("gpsimd", lambda e: None, wait=(("sg", k.c("sg")), ("sd", k.c("sd"))))
    return k if standalone else None if standalone else None



def _ident():
    return np.eye(128, dtype=np.float32)


def run_launch(k, in_maps, names):
    nc = k.build()
    res = run_bass_kernel_spmd(nc, in_maps, core_ids=list(range(NCORE)))
    return [{n: r[n] for n in names} for r in res.results]


def _rope_tables():
    RD, dh = 32, 64
    inv = 1.0 / 10000.0 ** (np.arange(0, RD, 2, dtype=np.float32) / RD)
    fr = np.arange(4096, dtype=np.float32)[:, None] * inv
    fr = np.repeat(fr, 2, axis=-1)
    cosd = np.ones((dh, 4096), np.float32); sind = np.zeros((dh, 4096), np.float32)
    cosd[:RD, :] = np.cos(fr).T; sind[:RD, :] = np.sin(fr).T
    P = np.zeros((dh, dh), np.float32)
    for i in range(RD // 2):
        P[2 * i, 2 * i + 1] = -1.0
        P[2 * i + 1, 2 * i] = 1.0
    return cosd, sind, np.ascontiguousarray(P.T)


def _cmask(c):
    m = np.zeros((4, 4, 8, 128, 512), np.float32)
    qi = np.arange(128)[:, None]
    ki = np.arange(512)[None, :]
    for qt in range(4):
        q0 = 512 * c + qt * 128
        for kb in range(8):
            k0 = kb * 512
            m[:, qt, kb] = np.where(k0 + ki <= q0 + qi, 0.0, -1e30)
    return m.astype(bfdt)


def _run(k, in_maps, names):
    nc = k.build()
    res = run_bass_kernel_spmd(nc, in_maps, core_ids=list(range(NCORE)))
    return [{n: np.asarray(r[n]) for n in names} for r in res.results]


def build_fused():
    k = KB("fused")
    RG = [list(range(NCORE))]
    io = {}
    d = io
    d["xsh"] = k.din("xsh", [T, DM], F32)
    d["n1w"] = k.din("n1w", [DM], F32); d["n1b"] = k.din("n1b", [DM], F32)
    d["wdin"] = k.din("wdin", [DM, 512], F32)
    d["n2w"] = k.din("n2w", [512], F32); d["n2b"] = k.din("n2b", [512], F32)
    d["crt"] = k.din("crt", [512, 64], F32); d["cit"] = k.din("cit", [512, 64], F32)
    d["LRi"] = k.din("LRi", [512], F32); d["LIi"] = k.din("LIi", [512], F32)
    d["Dh"] = k.din("Dh", [64], F32)
    d["Wxg"] = k.din("Wxg", [DM, 4096], F32); d["Wgate"] = k.din("Wgate", [512, 4096], F32)
    d["Wout"] = k.din("Wout", [4096, DM], F32); d["bout"] = k.din("bout", [DM], F32)
    d["gate_w"] = k.din("gate_w", [DM, 8], F32); d["gate_b"] = k.din("gate_b", [8], F32)
    d["w1"] = k.din("w1", [8, DM, 4096], F32); d["b1"] = k.din("b1", [8, 4096], F32)
    d["w2"] = k.din("w2", [8, 4096, DM], F32); d["b2"] = k.din("b2", [8, DM], F32)
    d["wqT"] = k.din("wqT", [DM, DM], F32); d["wkT"] = k.din("wkT", [DM, DM], F32)
    d["wvT"] = k.din("wvT", [DM, DM], F32); d["wrT"] = k.din("wrT", [DM, DM], F32)
    d["cosI"] = k.din("cosI", [64, 4096], F32); d["sinI"] = k.din("sinI", [64, 4096], F32)
    d["p64T"] = k.din("p64T", [64, 64], F32)
    d["gwT"] = k.din("gwT", [DM, 16], F32)
    d["w1f"] = k.din("w1f", [16, DM, 512], F32); d["w2f"] = k.din("w2f", [16, 512, DM], F32)
    d["w3f"] = k.din("w3f", [16, DM, 512], F32)
    d["ws1"] = k.din("ws1", [DM, DM], F32); d["ws2"] = k.din("ws2", [DM, DM], F32)
    d["ws3"] = k.din("ws3", [DM, DM], F32)
    d["w_gate"] = k.din("w_gate", [2816, DM], F32); d["w_up"] = k.din("w_up", [2816, DM], F32)
    d["w_down"] = k.din("w_down", [DM, 2816], F32)
    d["wg_slice"] = k.din("wg_slice", [384, DM], F32)
    d["wu_slice"] = k.din("wu_slice", [384, DM], F32)
    d["wd_slice"] = k.din("wd_slice", [128, 2816], F32)
    d["wT"] = k.din("wT", [DM, 2 * DM], F32)
    d["cmask"] = k.din("cmask", [4, 4, 8, 128, 512], BF16)
    d["router_w"] = k.din("router_w", [DM, 8], F32)
    d["w1e3"] = k.din("w1e3", [8, DM, DM], F32); d["w2e3"] = k.din("w2e3", [8, DM, DM], F32)
    d["sgu"] = k.din("sgu", [DM, 2 * DM], F32); d["sdwn"] = k.din("sdwn", [DM, DM], F32)
    d["sgw"] = k.din("sgw", [DM, 1], F32)
    y_o = k.dout("y_o", [T, DM], F32)

    xfm_d = k.dtmp("xfm_d", [DM, T], F32)
    xn0_d = k.dtmp("xn0_d", [DM, T], F32)
    z_d = k.dtmp("z_d", [DM, T], BF16)
    a2a1_in = k.dtmp("a2a1_in", [512, 2048], BF16)
    a2a1_out = k.dtmp("a2a1_out", [512, 2048], BF16)
    convin_d = k.dtmp("convin_d", [64, 128, 128], BF16)
    a2a2_in = k.dtmp("a2a2_in", [512, 2048], BF16)
    a2a2_out = k.dtmp("a2a2_out", [512, 2048], BF16)
    R1_d = k.dtmp("R1_d", [DM, T], F32)
    xn1_d = k.dtmp("xn1_d", [DM, T], F32)
    a2a3_in = k.dtmp("a2a3_in", [4096, 512], F32)
    a2a3_out = k.dtmp("a2a3_out", [4096, 512], F32)
    R2_d = k.dtmp("R2_d", [DM, T], F32)
    R3_d = k.dtmp("R3_d", [DM, T], F32)
    xn3_d = k.dtmp("xn3_d", [DM, T], F32)
    kag_in = k.dtmp("kag_in", [DM, T], BF16)
    kag_out = k.dtmp("kag_out", [NCORE * DM, T], BF16)
    vag_in = k.dtmp("vag_in", [T, DM], BF16)
    vag_out = k.dtmp("vag_out", [NCORE * T, DM], BF16)
    pv_d = k.dtmp("pv_d", [4, 512, DM], F32)

    cm = Common(k)

    def cc(kind, in_t, out_t, sem):
        k.sem(sem)
        k.on("gpsimd",
             (lambda e, i=in_t, o=out_t, kind=kind: e.collective_compute(
                 kind, ALU.bypass, replica_groups=RG, ins=[i.ap().opt()], outs=[o.ap().opt()])),
             wait=(("sg", k.c("sg")), ("sd", k.c("sd")), ("sv", k.c("sv"))),
             inc=((sem, 1),))

    with k.scope():
        build_l0_pre(k, cm, dict(io, xfm_o=xfm_d, xn_o=xn0_d, z_o=z_d, xn2_a2a=a2a1_in))
    cc("AllToAll", a2a1_in, a2a1_out, "cc1")
    for srcc in range(NCORE):
        k.dma("gpsimd",
              bass.AP(convin_d, srcc * 2048, [[128 * 128, 64], [128, 16], [1, 128]]),
              bass.AP(a2a1_out, srcc * 64 * 2048, [[2048, 64], [128, 16], [1, 128]]),
              wait=(("cc1", 1),))
    k.barrier(extra=(("cc1", 1),))
    with k.scope():
        build_l0_conv(k, cm, dict(io, conv_in=convin_d, y_a2a=a2a2_in))
    cc("AllToAll", a2a2_in, a2a2_out, "cc2")
    k.barrier(extra=(("cc2", 1),))
    with k.scope():
        build_l0_post(k, cm, dict(io, xfm_i=xfm_d, xn_i=xn0_d, z_i=z_d,
                                  d_a2a=a2a2_out, R1_o=R1_d))
    k.barrier()
    with k.scope():
        build_l1_attn(k, cm, dict(io, R1_i=R1_d, xn_o=xn1_d, x3_o=a2a3_in))
    cc("AllToAll", a2a3_in, a2a3_out, "cc3")
    r2 = k.dma("gpsimd", fm_ap(R2_d, 8), fm_ap(R1_d, 8), wait=(("cc3", 1),))
    for srcc in range(NCORE):
        r2 = k.dma("gpsimd",
                   bass.AP(R2_d, srcc * 128 * T, [[T, 128], [512, 4], [1, 512]]),
                   bass.AP(a2a3_out, srcc * 4 * 128 * 512, [[512, 128], [128 * 512, 4], [1, 512]]),
                   wait=(("sg", r2),), accum_op=ALU.add)
    k.barrier(extra=(("cc3", 1),))
    with k.scope():
        build_l1_ffn(k, cm, dict(io, xn_i=xn1_d, w1=io["w1f"], w2=io["w2f"], w3=io["w3f"],
                                 f1_o=R2_d), init_out=False, out_gate=r2)
    r2_ready = k.c("sg")
    k.barrier()
    with k.scope():
        build_l2(k, cm, dict(io, R3_o=R3_d, R2_d=R2_d), r2_ready=r2_ready)
    k.barrier()
    with k.scope():
        build_l3_pre(k, cm, dict(io, R3_i=R3_d, xn_o=xn3_d, K_o=kag_in, V_o=vag_in))
    cc("AllGather", kag_in, kag_out, "cc4")
    cc("AllGather", vag_in, vag_out, "cc5")
    k.barrier(extra=(("cc4", 1), ("cc5", 1)))
    with k.scope():
        build_l3_attn(k, cm, dict(io, xn3_d=xn3_d, kag=kag_out, vag=vag_out, pv_o=pv_d),
                      fused_kv=True)
    k.barrier()
    with k.scope():
        build_l3_post(k, cm, dict(io, R3_i=R3_d, xn3_i=xn3_d, pv_d=pv_d,
                                  w1=io["w1e3"], w2=io["w2e3"], y_o=y_o),
                      fused_pv=True)
    return k


_FUSED = {"k": None}


def kernel(**inputs):
    inp = inputs
    C = lambda a: np.ascontiguousarray(a)
    f32 = lambda a: np.asarray(a, np.float32)
    ident = np.eye(128, dtype=np.float32)
    x = f32(inp["L0_attn_idx"])
    cosd, sind, p64T = _rope_tables()
    k = build_fused()
    nc = k.build()
    common = dict(
        identI=ident, n1w=f32(inp["L0_attn_n1_w"]), n1b=f32(inp["L0_attn_n1_b"]),
        wdin=f32(inp["L0_attn_Wdin"]), n2w=f32(inp["L0_attn_n2_w"]), n2b=f32(inp["L0_attn_n2_b"]),
        LRi=f32(inp["L0_attn_LR"]), LIi=f32(inp["L0_attn_LI"]),
        Wxg=f32(inp["L0_attn_Wxg"]), Wgate=f32(inp["L0_attn_Wgate"]),
        Wout=f32(inp["L0_attn_Wout"]), bout=f32(inp["L0_attn_bout"]),
        gate_w=f32(inp["L0_ffn_gate_w"]), gate_b=f32(inp["L0_ffn_gate_b"]),
        w1=f32(inp["L0_ffn_w1"]), b1=f32(inp["L0_ffn_b1"]),
        w2=f32(inp["L0_ffn_w2"]), b2=f32(inp["L0_ffn_b2"]),
        wqT=C(f32(inp["L1_attn_Wq"]).T), wkT=C(f32(inp["L1_attn_Wk"]).T),
        wvT=C(f32(inp["L1_attn_Wv"]).T), wrT=C(f32(inp["L1_attn_Wr"]).T),
        cosI=cosd, sinI=sind, p64T=p64T,
        gwT=C(f32(inp["L1_ffn_gate_w"]).T),
        w1f=f32(inp["L1_ffn_w1"]), w2f=f32(inp["L1_ffn_w2"]), w3f=f32(inp["L1_ffn_w3"]),
        ws1=f32(inp["L1_ffn_ws1"]), ws2=f32(inp["L1_ffn_ws2"]), ws3=f32(inp["L1_ffn_ws3"]),
        w_gate=f32(inp["L2_ffn_w_gate"]), w_up=f32(inp["L2_ffn_w_up"]),
        w_down=f32(inp["L2_ffn_w_down"]),
        wT=C(f32(inp["L3_attn_W"]).T),
        router_w=f32(inp["L3_ffn_router_w"]),
        w1e3=f32(inp["L3_ffn_w1"]), w2e3=f32(inp["L3_ffn_w2"]),
        sgu=f32(inp["L3_ffn_shared_gate_up"]), sdwn=f32(inp["L3_ffn_shared_down"]),
        sgw=f32(inp["L3_ffn_shared_gate_w"]),
    )
    im = []
    for c in range(NCORE):
        m = dict(common)
        m["xsh"] = C(x[:, 512 * c:512 * (c + 1), :].reshape(T, DM))
        m["crt"] = C(f32(inp["L0_attn_CR"])[64 * c:64 * c + 64, :].T)
        m["cit"] = C(f32(inp["L0_attn_CI"])[64 * c:64 * c + 64, :].T)
        m["Dh"] = C(f32(inp["L0_attn_D"])[64 * c:64 * c + 64])
        m["cmask"] = _cmask(c)
        wg = f32(inp["L2_ffn_w_gate"]); wu = f32(inp["L2_ffn_w_up"])
        wgp = np.zeros((3072, DM), np.float32); wgp[:2816] = wg
        wup = np.zeros((3072, DM), np.float32); wup[:2816] = wu
        m["wg_slice"] = C(wgp[384 * c:384 * c + 384])
        m["wu_slice"] = C(wup[384 * c:384 * c + 384])
        m["wd_slice"] = C(f32(inp["L2_ffn_w_down"])[128 * c:128 * c + 128])
        im.append(m)
    res = run_bass_kernel_spmd(nc, im, core_ids=list(range(NCORE)))
    y = np.zeros((B, L, DM), np.float32)
    for c in range(NCORE):
        y[:, 512 * c:512 * (c + 1), :] = np.asarray(res.results[c]["y_o"]).reshape(B, 512, DM)
    return y
